# version 03
# baseline (speedup 1.0000x reference)
import numpy as np
import ml_dtypes
from contextlib import ExitStack

import concourse.bass as bass
import concourse.mybir as mybir
from concourse.bass_utils import run_bass_kernel_spmd

F32 = mybir.dt.float32
BF16 = mybir.dt.bfloat16
I32 = mybir.dt.int32
AF = mybir.ActivationFunctionType
ALU = mybir.AluOpType
AX = mybir.AxisListType

B_, T_, DIM = 2, 8192, 1024
DEPTH, HEADS, DH = 4, 8, 128
NLOCAL, WINDOW, BUCKET = 4, 128, 64
FF = 4096
E8, HID, RANK = 8, 2048, 16
T0 = 5632
T1 = T_ - T0
KD = DIM // 128
QSCALE = float(DH) ** -0.5
NEG = -1.0e30
EPS = 1.0e-5

ENGINES = ("tensor", "vector", "scalar", "gpsimd", "sync")

TEST_MODE = None
DEBUG_DUMP = False
CD_WITH_MOE = False


def ts(i, n):
    return slice(i * n, (i + 1) * n)


class Ctx:
    DMA_SEMS = ("sd", "sg", "sq")

    def __init__(self, engine_name, eng, sems, bufs, inp):
        self.engine = engine_name
        self.e = eng
        self.sems = sems
        self.cnt = {k: 0 for k in sems}
        self.hw = {k: 0 for k in sems}
        self.guard = {k: 0 for k in sems}
        self.b = bufs
        self.inp = inp
        self.misc = {}

    def wait(self, engine, sem, val=None):
        if val is None:
            val = self.cnt[sem]
        if val <= 0:
            return
        self.hw[sem] = max(self.hw[sem], val)
        if self.engine == engine:
            self.e.wait_ge(self.sems[sem], val)

    def bump(self, sem, n=1):
        self.cnt[sem] += n
        return self.cnt[sem]

    SELF_SEM = {"vector": "ev", "scalar": "es", "gpsimd": "eg"}

    def op(self, engine, fn, inc=None, inc_n=1):
        s = self.SELF_SEM.get(engine)
        if s is not None:
            if self.engine == engine and self.cnt[s] > 0:
                self.e.wait_ge(self.sems[s], self.cnt[s])
            if inc is None:
                inc, inc_n = s, 1
        inst = None
        if self.engine == engine:
            inst = fn(self.e)
        if inc is not None:
            self.bump(inc, inc_n)
            if inst is not None:
                inst.then_inc(self.sems[inc], inc_n)
        return inst

    def ckpt(self, sem):
        self.hw[sem] = max(self.hw[sem], self.cnt[sem])
        return self.cnt[sem]

    def _dma_guard(self, queue, sem):
        if self.hw[sem] > self.guard[sem]:
            tgt = self.hw[sem]
            if self.engine == queue:
                self.e.wait_ge(self.sems[sem], tgt)
            self.guard[sem] = tgt

    def dma(self, queue, out, in_, sem=None, **kw):
        if sem is None:
            sem = {"sync": "sd", "gpsimd": "sg", "scalar": "sq"}[queue]
        self._dma_guard(queue, sem)
        return self.op(queue, lambda e: e.dma_start(out=out, in_=in_, **kw),
                       inc=sem, inc_n=16)

    def dma_tr(self, out, in_, sem="sd"):
        self._dma_guard("sync", sem)
        return self.op("sync", lambda e: e.dma_start_transpose(out, in_),
                       inc=sem, inc_n=16)


def emit_xn(ctx, n_tok, want_r, want_xnT):
    b = ctx.b
    x_in = ctx.inp["x_in"]
    ntt = n_tok // 128
    free = [0, 0]
    for i in range(ntt):
        pb = i % 2
        xt, xf, xb16 = b["lnx"][pb], b["lnxf"][pb], b["lnxb"][pb]
        ctx.wait("sync", "ev", free[pb])
        ctx.dma("sync", xt[:, :], x_in[ts(i, 128), :])
        ctx.wait("vector", "sd")
        s1, s2, mm_, rs, xsq = b["ln_s1"], b["ln_s2"], b["ln_mm"], b["ln_rs"], b["lnsq"]
        ctx.op("vector", lambda e: e.tensor_mul(xsq[:, :], xt[:, :], xt[:, :]))
        ctx.op("vector", lambda e: e.reduce_sum(s1[:, :], xt[:, :], axis=AX.X))
        ctx.op("vector", lambda e: e.reduce_sum(s2[:, :], xsq[:, :], axis=AX.X))
        ctx.op("vector", lambda e: e.tensor_scalar_mul(s1[:, :], s1[:, :], 1.0 / DIM))
        ctx.op("vector", lambda e: e.tensor_mul(mm_[:, :], s1[:, :], s1[:, :]))
        ctx.op("vector", lambda e: e.tensor_scalar(s2[:, :], s2[:, :], 1.0 / DIM, EPS, ALU.mult, ALU.add))
        ctx.op("vector", lambda e: e.tensor_sub(s2[:, :], s2[:, :], mm_[:, :]))
        ctx.op("vector", lambda e: e.reciprocal(s2[:, :], s2[:, :]), inc="ev")
        ctx.wait("scalar", "ev")
        ctx.op("scalar", lambda e: e.activation(rs[:, :], s2[:, :], AF.Sqrt), inc="es")
        ctx.wait("vector", "es")
        ctx.op("vector", lambda e: e.tensor_scalar(xf[:, :], xt[:, :], s1[:, :1], rs[:, :1], ALU.subtract, ALU.mult))
        ctx.op("vector", lambda e: e.tensor_copy(xb16[:, :], xf[:, :]), inc="ev")
        free[pb] = ctx.cnt["ev"]
        ctx.wait("sync", "ev")
        ctx.dma("sync", b["xq_dram"].ap()[ts(i, 128), :], xb16[:, :])
        if want_r:
            ctx.dma("sync", b["r_dram"].ap()[ts(i, 128), :], xf[:, :])
    if want_xnT:
        ctx.wait("sync", "sd")
        tfree = [0, 0]
        for k in range(KD):
            tb = b["xnTt"][k % 2]
            ctx.wait("sync", "sd")
            ctx.dma_tr(tb[:, :n_tok], b["xq_dram"].ap()[:n_tok, ts(k, 128)])
            ctx.wait("sync", "sd")
            ctx.dma("sync", b["xnT_dram"].ap()[ts(k, 128), :n_tok], tb[:, :n_tok])
            tfree[k % 2] = ctx.cnt["sd"]
        ctx.wait("sync", "sd")


def emit_zero_out(ctx, row0, row1):
    b = ctx.b
    z = b["zero1k"]
    ctx.op("gpsimd", lambda e: e.memset(z[:, :], 0.0), inc="eg")
    ctx.wait("gpsimd", "eg")
    for i in range(row0 // 128, row1 // 128):
        ctx.dma("gpsimd", ctx.inp["out"][ts(i, 128), :], z[:, :])


def emit_moe(ctx, n_tok, g_dst, gflag_dst):
    b = ctx.b
    nsub = n_tok // 512
    xnT = b["xnT_dram"]
    w1T, w3T, w2T = b["w1T"], b["w3T"], b["w2T"]
    psb = b["psb"]

    ctx.dma("sync", b["b1c"][:, :], ctx.inp["m_b1c"][:, :])
    ctx.dma("sync", b["b3c"][:, :], ctx.inp["m_b3c"][:, :])
    ctx.dma("sync", b["b2nat"][:8, :], ctx.inp["m_b2"][:, :])
    ctx.dma("sync", b["wgT"][:, :], ctx.inp["m_wgT"][:, :])
    ctx.dma("gpsimd", b["a13"][:, 0:1024], ctx.inp["L0_ffn_A1"].rearrange("e r d -> (e r) d"))
    ctx.dma("gpsimd", b["a13"][:, 1024:2048], ctx.inp["L0_ffn_A3"].rearrange("e r d -> (e r) d"))
    ctx.wait("tensor", "sd")
    ctx.wait("tensor", "sg")
    for k in range(KD):
        ctx.op("tensor", lambda e, k=k: e.transpose(b["psbf"][:, 0:128], b["a13"][:, ts(k, 128)], b["identb"][:, :]))
        ctx.op("tensor", lambda e, k=k: e.transpose(b["psbf"][:, 128:256], b["a13"][:, 1024 + k * 128:1024 + (k + 1) * 128], b["identb"][:, :]), inc="ct")
        ctx.wait("vector", "ct")
        ctx.op("vector", lambda e, k=k: e.tensor_copy(b["A13T"][:, ts(k, 256)], b["psbf"][:, 0:256]))
        ctx.wait("tensor", "ev")

    xnS_free = [0, 0]
    for s in range(nsub):
        xnS = b["xnS"][s % 2]
        ctx.wait("sync", "ct", xnS_free[s % 2])
        for k in range(KD):
            ctx.dma("sync", xnS[:, ts(k, 512)], xnT.ap()[ts(k, 128), ts(s, 512)])
        ctx.wait("tensor", "sd")
        for mt in range(4):
            i = s * 4 + mt
            ps = psb[mt % 2]
            for k in range(KD):
                ctx.op("tensor", lambda e, k=k, mt=mt, ps=ps, xnS=xnS: e.matmul(
                    ps[:, 0:8], xnS[:, k * 512 + mt * 128:k * 512 + (mt + 1) * 128],
                    b["wgT"][:, ts(k, 8)], start=(k == 0), stop=(k == KD - 1)),
                    inc=("ct" if k == KD - 1 else None))
            ctx.wait("vector", "ct")
            lg = b["lg"]
            m1n, m1p, m2p, gsc = b["g_m1n"], b["g_m1p"], b["g_m2p"], b["g_sc"]
            ctx.op("vector", lambda e, ps=ps: e.tensor_copy(lg[:, :], ps[:, 0:8]))
            ctx.wait("tensor", "ev")
            ctx.op("vector", lambda e: e.reduce_max(m1n[:, :], lg[:, :], axis=AX.X, negate=True))
            ctx.op("vector", lambda e: e.tensor_scalar_mul(m1p[:, :], m1n[:, :], -1.0))
            ctx.op("vector", lambda e: e.tensor_single_scalar(b["g_eq"][:, :], lg[:, :], m1p[:, :1], ALU.is_equal))
            ctx.op("vector", lambda e: e.scalar_tensor_tensor(b["g_l2"][:, :], b["g_eq"][:, :], NEG, lg[:, :], ALU.mult, ALU.add))
            ctx.op("vector", lambda e: e.reduce_max(m2p[:, :], b["g_l2"][:, :], axis=AX.X))
            ctx.op("vector", lambda e: e.tensor_single_scalar(b["g_ind"][:, :], lg[:, :], m2p[:, :1], ALU.is_ge))
            ctx.wait("scalar", "ev")
            ctx.op("scalar", lambda e: e.activation(b["g_ex"][:, :], lg[:, :], AF.Exp, bias=m1n[:, :1], scale=1.0))
            ctx.wait("vector", "es")
            ctx.op("vector", lambda e: e.tensor_mul(b["g_ex"][:, :], b["g_ex"][:, :], b["g_ind"][:, :]))
            ctx.op("vector", lambda e: e.reduce_sum(gsc[:, :], b["g_ex"][:, :], axis=AX.X))
            ctx.op("vector", lambda e: e.reciprocal(gsc[:, :], gsc[:, :]))
            ctx.op("vector", lambda e, i=i: e.tensor_scalar_mul(b["gates"][:, ts(i, 8)], b["g_ex"][:, :], gsc[:, :1]))
            ctx.wait("tensor", "ev")
            ctx.op("tensor", lambda e, i=i: e.transpose(b["psf_t2"][:8, 0:128], b["gates"][:, ts(i, 8)], b["identf"][:, :]), inc="ct")
            ctx.wait("vector", "ct")
            ctx.op("vector", lambda e, i=i, mt=mt: e.tensor_copy(b["gT_sb"][:8, ts(mt, 128)], b["psf_t2"][:8, 0:128]))
            ctx.wait("tensor", "ev")
        ctx.wait("sync", "ev")
        ctx.dma("sync", b["gatesT_dram"].ap()[:, ts(s, 512)], b["gT_sb"][:8, 0:512])
        for m in range(2):
            ps = psb[m % 2]
            for k in range(KD):
                ctx.op("tensor", lambda e, k=k, m=m, ps=ps, xnS=xnS: e.matmul(
                    ps[:, 0:512], b["A13T"][:, k * 256 + m * 128:k * 256 + (m + 1) * 128],
                    xnS[:, ts(k, 512)], start=(k == 0), stop=(k == KD - 1)),
                    inc=("ct" if k == KD - 1 else None))
            ctx.wait("vector", "ct")
            ctx.op("vector", lambda e, m=m, ps=ps: e.tensor_copy(b["ter_sb"][:, ts(m, 512)], ps[:, 0:512]))
            ctx.wait("tensor", "ev")
        ctx.wait("sync", "ev")
        ctx.dma("sync", b["terd"].ap()[:, ts(s, 512)].rearrange("(m p) t -> p m t", p=128),
                b["ter_sb"][:, 0:1024].rearrange("p (m t) -> p m t", t=512))
        xnS_free[s % 2] = ctx.cnt["ct"]

    wsrc = {"w1": ctx.inp["L0_ffn_W1"], "w3": ctx.inp["L0_ffn_W3"], "w2": ctx.inp["L0_ffn_W2"]}
    wdram = {"w1": b["w1b_dram"], "w3": b["w3b_dram"], "w2": b["w2b_dram"]}
    wsb = {"w1": w1T, "w3": w3T, "w2": w2T}
    stage = b["wstage"]
    stage_free = [0]
    cast_done = {}
    wT_ready = {}
    wfree = {(w, k): 0 for w in ("w1", "w3") for k in range(KD)}
    wfree.update({("w2", k): 0 for k in range(16)})
    ydone = [0] * nsub
    ysub_free = [0]
    lora_free = [0]

    def cast_weights(e_idx):
        for wname in ("w1", "w3", "w2"):
            rows = HID if wname != "w2" else DIM
            cols = DIM if wname != "w2" else HID
            qr = rows // 8
            for h in range(8):
                sb_ = stage
                ctx.wait("gpsimd", "sd", stage_free[0])
                ctx.dma("gpsimd", sb_[:, : qr * cols // 128].rearrange("p (a c) -> p a c", c=cols),
                        wsrc[wname][e_idx, h * qr:(h + 1) * qr, :].rearrange("(a p) c -> p a c", p=128))
                ctx.wait("sync", "sg")
                ctx.dma("sync", wdram[wname].ap()[h * qr:(h + 1) * qr, :].rearrange("(a p) c -> p a c", p=128),
                        sb_[:, : qr * cols // 128].rearrange("p (a c) -> p a c", c=cols))
                stage_free[0] = ctx.ckpt("sd")
        cast_done[e_idx] = ctx.ckpt("sd")

    def load_wT(e_idx):
        ctx.wait("sync", "sd", cast_done.get(e_idx, 0))
        for wname in ("w1", "w3"):
            for k in range(KD):
                ctx.wait("sync", "ct", wfree[(wname, k)])
                ctx.dma_tr(wsb[wname][:, ts(k, HID)], wdram[wname].ap()[:, ts(k, 128)])
                wT_ready[(wname, k)] = ctx.ckpt("sd")
        for j in range(16):
            ctx.wait("sync", "ct", wfree[("w2", j)])
            ctx.dma_tr(wsb["w2"][:, ts(j, DIM)], wdram["w2"].ap()[:, ts(j, 128)])
            wT_ready[("w2", j)] = ctx.ckpt("sd")

    cast_weights(0)
    load_wT(0)

    for e_idx in range(E8):
        ctx.wait("sync", "ct", lora_free[0])
        ctx.dma("sync", b["B1Te"][:16, :], ctx.inp["m_B1t"][:, ts(e_idx, HID)])
        ctx.dma("sync", b["B3Te"][:16, :], ctx.inp["m_B3t"][:, ts(e_idx, HID)])
        ctx.dma("sync", b["B2Te"][:16, :], ctx.inp["m_B2t"][:, ts(e_idx, DIM)])
        ctx.dma("sync", b["A2Te"][:, :], ctx.inp["m_A2t"][:, ts(e_idx, 256)])
        for s in range(nsub):
            xnS = b["xnS"][s % 2]
            ctx.wait("sync", "ct", xnS_free[s % 2])
            for k in range(KD):
                ctx.dma("sync", xnS[:, ts(k, 512)], xnT.ap()[ts(k, 128), ts(s, 512)])
            ctx.wait("sync", "ct", lora_free[0])
            ctx.dma("sync", b["terAe"][:16, :], b["terd"].ap()[e_idx * 16:(e_idx + 1) * 16, ts(s, 512)])
            ctx.dma("sync", b["terBe"][:16, :], b["terd"].ap()[128 + e_idx * 16:128 + (e_idx + 1) * 16, ts(s, 512)])
            ctx.wait("tensor", "sd")
            for m in range(16):
                for which in ("h1", "h3"):
                    wT_ = w1T if which == "h1" else w3T
                    Bt = b["B1Te"] if which == "h1" else b["B3Te"]
                    tere = b["terAe"] if which == "h1" else b["terBe"]
                    bc = b["b1c"] if which == "h1" else b["b3c"]
                    tmp = b["tmp1"] if which == "h1" else b["tmp3"]
                    ps = psb[0 if which == "h1" else 1]
                    ctx.wait("tensor", "sd", wT_ready[("w1" if which == "h1" else "w3", KD - 1)])
                    for k in range(KD):
                        ctx.op("tensor", lambda e, k=k, m=m, wT_=wT_, ps=ps, xnS=xnS: e.matmul(
                            ps[:, 0:512], wT_[:, k * HID + m * 128:k * HID + (m + 1) * 128],
                            xnS[:, ts(k, 512)], start=(k == 0), stop=False))
                    ctx.op("tensor", lambda e, m=m, Bt=Bt, tere=tere, ps=ps: e.matmul(
                        ps[:, 0:512], Bt[:16, ts(m, 128)],
                        tere[:16, 0:512], start=False, stop=True), inc="ct")
                    bcol = bc[:, e_idx * 16 + m:e_idx * 16 + m + 1]
                    if which == "h1":
                        ctx.wait("scalar", "ct")
                        ctx.op("scalar", lambda e, bcol=bcol, tmp=tmp, ps=ps: e.activation(
                            tmp[:, :], ps[:, 0:512], AF.Silu, bias=bcol, scale=1.0))
                        ctx.wait("tensor", "es")
                    else:
                        ctx.wait("vector", "ct")
                        ctx.op("vector", lambda e, bcol=bcol, tmp=tmp, ps=ps: e.tensor_scalar_add(
                            tmp[:, :], ps[:, 0:512], bcol[:, :1]))
                        ctx.wait("tensor", "ev")
                ctx.wait("vector", "es")
                ctx.op("vector", lambda e, m=m: e.tensor_mul(b["gsb"][:, ts(m, 512)], b["tmp1"][:, :], b["tmp3"][:, :]))
                ctx.wait("scalar", "ev")
            for k in range(KD):
                wfree[("w1", k)] = ctx.cnt["ct"]
                wfree[("w3", k)] = ctx.cnt["ct"]
            ps2 = b["psf_t2"]
            ctx.wait("tensor", "ev")
            for j in range(16):
                ctx.op("tensor", lambda e, j=j: e.matmul(
                    ps2[:16, 0:512], b["A2Te"][:, ts(j, 16)],
                    b["gsb"][:, ts(j, 512)], start=(j == 0), stop=(j == 15)),
                    inc=("ct" if j == 15 else None))
            ctx.wait("vector", "ct")
            ctx.op("vector", lambda e: e.tensor_copy(b["ter2"][:16, :], ps2[:16, 0:512]))
            ctx.wait("tensor", "ev")
            ctx.wait("tensor", "sd", wT_ready[("w2", 15)])
            for mt in range(4):
                i = s * 4 + mt
                for n_ in range(2):
                    ps = psb[(mt * 2 + n_) % 2]
                    for k in range(16):
                        ctx.op("tensor", lambda e, k=k, mt=mt, n_=n_, ps=ps: e.matmul(
                            ps[:, 0:512], b["gsb"][:, k * 512 + mt * 128:k * 512 + (mt + 1) * 128],
                            w2T[:, k * DIM + n_ * 512:k * DIM + (n_ + 1) * 512],
                            start=(k == 0), stop=False))
                    ctx.op("tensor", lambda e, mt=mt, n_=n_, ps=ps: e.matmul(
                        ps[:, 0:512], b["ter2"][:16, ts(mt, 128)],
                        b["B2Te"][:16, ts(n_, 512)], start=False, stop=True), inc="ct")
                    ctx.wait("scalar", "ct")
                    ctx.wait("scalar", "sg", ysub_free[0])
                    gcol = b["gates"][:, i * 8 + e_idx:i * 8 + e_idx + 1]
                    ctx.op("scalar", lambda e, mt=mt, n_=n_, gcol=gcol, ps=ps: e.activation(
                        b["ysub"][:, mt * 1024 + n_ * 512:mt * 1024 + (n_ + 1) * 512],
                        ps[:, 0:512], AF.Copy, bias=0.0, scale=gcol))
                    ctx.wait("tensor", "es")
            for k in range(16):
                wfree[("w2", k)] = ctx.cnt["ct"]
            lora_free[0] = ctx.cnt["ct"]
            xnS_free[s % 2] = ctx.cnt["ct"]
            ctx.wait("gpsimd", "es")
            ctx.wait("gpsimd", "sg", ydone[s])
            kw = {} if e_idx == 0 else {"accum_op": ALU.add}
            ctx.dma("gpsimd", b["y_dram"].ap()[ts(s, 512), :].rearrange("(a p) c -> p a c", p=128),
                    b["ysub"][:, 0:4096].rearrange("p (a c) -> p a c", c=1024), **kw)
            ydone[s] = ctx.ckpt("sg")
            ysub_free[0] = ydone[s]
        if e_idx + 1 < E8:
            cast_weights(e_idx + 1)
            load_wT(e_idx + 1)

    gship_free = 0
    ctx.ckpt("sg")
    ctx.wait("sync", "sg")
    for s in range(nsub):
        yb = b["ysub"]
        ctx.wait("sync", "es", gship_free)
        ctx.dma("sync", yb[:, 0:4096].rearrange("p (a c) -> p a c", c=1024),
                b["y_dram"].ap()[ts(s, 512), :].rearrange("(a p) c -> p a c", p=128))
        ctx.dma("sync", b["gT_sb"][:8, 0:512], b["gatesT_dram"].ap()[:, ts(s, 512)])
        ctx.wait("tensor", "sd")
        for mt in range(4):
            for n_ in range(2):
                ps = psb[(mt * 2 + n_) % 2]
                ctx.op("tensor", lambda e, mt=mt, n_=n_, ps=ps: e.matmul(
                    ps[:, 0:512], b["gT_sb"][:8, ts(mt, 128)], b["b2nat"][:8, ts(n_, 512)],
                    start=True, stop=True), inc="ct")
                ctx.wait("vector", "ct")
                ctx.op("vector", lambda e, mt=mt, n_=n_, ps=ps: e.tensor_tensor(
                    yb[:, mt * 1024 + n_ * 512:mt * 1024 + (n_ + 1) * 512],
                    yb[:, mt * 1024 + n_ * 512:mt * 1024 + (n_ + 1) * 512],
                    ps[:, 0:512], ALU.add))
                ctx.wait("tensor", "ev")
        for hh in range(2):
            ctx.wait("scalar", "ev")
            ctx.wait("scalar", "sq")
            ctx.op("scalar", lambda e, hh=hh: e.activation(b["gout"][:, 0:2048], yb[:, hh * 2048:(hh + 1) * 2048], AF.Silu))
            ctx.wait("scalar", "es")
            ctx.dma("scalar", g_dst[s * 512 + hh * 256:s * 512 + (hh + 1) * 256, :].rearrange("(a p) c -> p a c", p=128),
                    b["gout"][:, 0:2048].rearrange("p (a c) -> p a c", c=1024))
        gship_free = ctx.cnt["es"]
        ctx.wait("gpsimd", "sq")
        ctx.wait("gpsimd", "sg")
        ctx.op("gpsimd", lambda e, s=s: e.memset(b["flag_sb"][:1, :], MAGIC + s + 1))
        ctx.wait("gpsimd", "eg")
        ctx.dma("gpsimd", gflag_dst[:1, :], b["flag_sb"][:1, :])
    ctx.wait("gpsimd", "sg")
    ctx.wait("sync", "sd")
    ctx.wait("scalar", "sq")
    ctx.wait("sync", "sq")
    ctx.wait("sync", "es")
    ctx.wait("sync", "eg")
    ctx.wait("sync", "sg")
    ctx.wait("vector", "sq")
    ctx.wait("gpsimd", "sq")
    ctx.wait("scalar", "sg")



MAGIC = 0x5AFE0000


def emit_poll(ctx, flag_ap, target, widx):
    b = ctx.b
    if ctx.engine == "gpsimd":
        e = ctx.e
        with e.register(f"pfv{widx}") as fv, e.register(f"pt{widx}") as t1, \
             e.register(f"pc1{widx}") as c1, e.register(f"pc2{widx}") as c2, \
             e.register(f"pk{widx}") as ok:
            e.reg_mov(fv, 0)

            def cond():
                e.reg_alu(t1, fv, MAGIC, ALU.subtract)
                e.reg_alu(c1, t1, target, ALU.is_ge)
                e.reg_alu(c2, t1, 64, ALU.is_le)
                e.reg_alu(ok, c1, c2, ALU.bitwise_and)
                e.reg_alu(ok, ok, 1, ALU.bitwise_xor)
                return ok

            with e.While(cond):
                e.reg_add(b["pollreg"], b["pollreg"], 16)
                e.dma_start(out=b["flag2_sb"][:1, :], in_=flag_ap[:1, :]).then_inc(ctx.sems["sp"], 16)
                e.wait_ge(ctx.sems["sp"], b["pollreg"])
                e.reg_load(fv, b["flag2_sb"][:1, 0:1])
        e.sem_inc(ctx.sems["ip"], 1)
    ctx.bump("ip")


def emit_combine(ctx, n_tok, g_src):
    b = ctx.b
    ctx.wait("vector", "ip")
    ctx.wait("sync", "ip")
    ctx.wait("sync", "sd")
    free = [0, 0]
    for i in range(n_tok // 128):
        pb = i % 2
        rt, gt, xt = b["cmb_r"][pb], b["cmb_g"][pb], b["cmb_x"][pb]
        ctx.wait("sync", "sq", free[pb])
        ctx.dma("sync", rt[:, :], b["r_dram"].ap()[ts(i, 128), :])
        ctx.dma("sync", gt[:, :], g_src[ts(i, 128), :])
        ctx.dma("sync", xt[:, :], ctx.inp["x_in"][ts(i, 128), :])
        ctx.wait("vector", "sd")
        ctx.op("vector", lambda e, rt=rt, gt=gt: e.tensor_mul(rt[:, :], rt[:, :], gt[:, :]))
        ctx.op("vector", lambda e, rt=rt, xt=xt: e.tensor_add(rt[:, :], rt[:, :], xt[:, :]))
        ctx.wait("scalar", "ev")
        ctx.dma("scalar", ctx.inp["out"][ts(i, 128), :], rt[:, :])
        free[pb] = ctx.ckpt("sq")
    ctx.wait("scalar", "sq")
    ctx.wait("sync", "sq")


def _emit_ln_pass(ctx, n_tok, dst_dram):
    b = ctx.b
    ctx.wait("sync", "sd")
    free = [0, 0]
    for i in range(n_tok // 128):
        pb = i % 2
        xt, xb16 = b["alx"][pb], b["alxb"][pb]
        ctx.wait("sync", "ev", free[pb])
        ctx.dma("sync", xt[:, :], b["r_dram"].ap()[ts(i, 128), :])
        ctx.wait("vector", "sd")
        s1, s2, mm_, rs, xsq = b["ln_s1"], b["ln_s2"], b["ln_mm"], b["ln_rs"], b["alsq"]
        ctx.op("vector", lambda e, xt=xt: e.tensor_mul(xsq[:, :], xt[:, :], xt[:, :]))
        ctx.op("vector", lambda e, xt=xt: e.reduce_sum(s1[:, :], xt[:, :], axis=AX.X))
        ctx.op("vector", lambda e: e.reduce_sum(s2[:, :], xsq[:, :], axis=AX.X))
        ctx.op("vector", lambda e: e.tensor_scalar_mul(s1[:, :], s1[:, :], 1.0 / DIM))
        ctx.op("vector", lambda e: e.tensor_mul(mm_[:, :], s1[:, :], s1[:, :]))
        ctx.op("vector", lambda e: e.tensor_scalar(s2[:, :], s2[:, :], 1.0 / DIM, EPS, ALU.mult, ALU.add))
        ctx.op("vector", lambda e: e.tensor_sub(s2[:, :], s2[:, :], mm_[:, :]))
        ctx.op("vector", lambda e: e.reciprocal(s2[:, :], s2[:, :]))
        ctx.wait("scalar", "ev")
        ctx.op("scalar", lambda e: e.activation(rs[:, :], s2[:, :], AF.Sqrt))
        ctx.wait("vector", "es")
        ctx.op("vector", lambda e, xt=xt, xb16=xb16: e.tensor_scalar(
            xb16[:, :], xt[:, :], s1[:, :1], rs[:, :1], ALU.subtract, ALU.mult))
        free[pb] = ctx.cnt["ev"]
        ctx.wait("sync", "ev")
        ctx.dma("sync", dst_dram.ap()[ts(i, 128), :], xb16[:, :])
    ctx.wait("sync", "sd")


def _load_and_prep_proj(ctx, L):
    b = ctx.b
    psb = b["psb"]
    wq, wk, wv, wo = b["Wq"], b["Wk"], b["Wv"], b["Wo"]
    ctx.wait("gpsimd", "ct", ctx.misc.setdefault("_wfree", [0])[0])
    ctx.wait("gpsimd", "sd")
    ctx.wait("gpsimd", "ev")
    ctx.wait("gpsimd", "es")
    ctx.wait("gpsimd", "sq")
    for wt, nm in ((wq, "L0_attn_Wq"), (wk, "L0_attn_Wk"), (wv, "L0_attn_Wv"), (wo, "L0_attn_Wo")):
        ctx.dma("gpsimd", wt[:, :].rearrange("p (a c) -> p a c", c=DIM),
                ctx.inp[nm][L].rearrange("(a p) c -> p a c", p=128))
    ctx.wait("sync", "ct", ctx.misc.setdefault("_wfree", [0])[0])
    ctx.dma("sync", b["abor"][:1, :], ctx.inp["m_abor"][:, ts(L, DIM)])
    ctx.wait("tensor", "sg")
    ctx.wait("tensor", "sd")
    for wt, brow in ((wq, b["bqr"]), (wk, b["bkr"]), (wv, b["bvr"])):
        for n_ in range(2):
            ps = psb[n_ % 2]
            for k in range(KD):
                ctx.op("tensor", lambda e, k=k, n_=n_, wt=wt, ps=ps, L=L: e.matmul(
                    ps[:1, 0:512], b["be1c"][:, L * 8 + k:L * 8 + k + 1],
                    wt[:, k * DIM + n_ * 512:k * DIM + (n_ + 1) * 512],
                    start=(k == 0), stop=(k == KD - 1)), inc=("ct" if k == KD - 1 else None))
            ctx.wait("vector", "ct")
            ctx.op("vector", lambda e, n_=n_, brow=brow, ps=ps: e.tensor_copy(brow[:1, ts(n_, 512)], ps[:1, 0:512]))
            ctx.wait("tensor", "ev")
    for brow, bcol in ((b["bqr"], b["bqc"]), (b["bkr"], b["bkc"])):
        for k in range(4):
            ps = psb[k % 2]
            ctx.op("tensor", lambda e, k=k, brow=brow, ps=ps: e.matmul(
                ps[:, 0:1], brow[:1, ts(k, 128)], b["ones1"][:1, 0:1],
                start=True, stop=True), inc="ct")
            ctx.wait("vector", "ct")
            ctx.op("vector", lambda e, k=k, bcol=bcol, ps=ps: e.tensor_copy(bcol[:, k:k + 1], ps[:, 0:1]))
            ctx.wait("tensor", "ev")
    for wt in (wq, wk, wv):
        for k in range(KD):
            ctx.op("vector", lambda e, k=k, wt=wt, L=L: e.tensor_scalar_mul(
                wt[:, ts(k, DIM)], wt[:, ts(k, DIM)], b["g1c"][:, L * 8 + k:L * 8 + k + 1]))
    ctx.wait("tensor", "ev")
    ctx.wait("gpsimd", "ev")


def emit_attn(ctx, n_tok, import_states, export=True, nlayers=DEPTH):
    b = ctx.b
    nch = n_tok // 512
    psb = b["psb"]
    state_ps = [b["p0"], b["p1"], b["p2"], b["p3"]]

    ctx.dma("sync", b["g1c"][:, :], ctx.inp["m_ln1g"][:, :])
    ctx.dma("sync", b["be1c"][:, :], ctx.inp["m_ln1b"][:, :])
    ctx.dma("sync", b["g2c"][:, :], ctx.inp["m_ln2g"][:, :])
    ctx.dma("sync", b["be2c"][:, :], ctx.inp["m_ln2b"][:, :])
    ctx.dma("sync", b["ab1c"][:, :], ctx.inp["m_ab1c"][:, :])
    ctx.dma("sync", b["maskS"][:, :], ctx.inp["m_maskS"][:, :])
    ctx.dma("sync", b["maskF"][:, :], ctx.inp["m_maskF"][:, :])
    ctx.op("gpsimd", lambda e: e.memset(b["ones1"][:1, :], 1.0))
    ctx.op("gpsimd", lambda e: e.memset(b["onescol"][:, :], 1.0))
    ctx.wait("tensor", "sd")
    ctx.wait("tensor", "eg")
    ctx.wait("vector", "eg")
    ctx.wait("vector", "sd")
    ctx.wait("scalar", "sd")

    exp_free = 0

    for L in range(nlayers):
        _emit_ln_pass(ctx, n_tok, b["xq_dram"])
        _load_and_prep_proj(ctx, L)

        ctx.wait("vector", "ct")
        ctx.wait("sync", "ct")
        ctx.wait("sync", "ev")
        ctx.wait("vector", "sd", exp_free)
        if import_states:
            emit_poll(ctx, b["sflag"].ap(), L + 1, 100 + L)
            ctx.wait("sync", "ip")
            ctx.dma("sync", b["ctx_imp"][:, 0:512], b["st_state"].ap()[ts(L, 128), 0:512])
            ctx.dma("sync", b["z_imp"][:, 0:4], b["st_state"].ap()[ts(L, 128), 512:516])
            ctx.dma("sync", b["kT"][:, :].rearrange("p (h c) -> p h c", c=640)[:, :, 0:128],
                    b["st_khalo"].ap()[ts(L, 128), :].rearrange("p (h c) -> p h c", c=128))
            ctx.dma("sync", b["v_ext"][:, 0:1024], b["st_vhalo"].ap()[ts(L, 128), :])
            ctx.wait("vector", "sd")
            ctx.wait("tensor", "sd")
        else:
            ctx.op("vector", lambda e: e.memset(b["ctx_imp"][:, 0:512], 0.0))
            ctx.op("vector", lambda e: e.memset(b["z_imp"][:, 0:4], 0.0))
            ctx.op("vector", lambda e: e.memset(b["kT"][:, :].rearrange("p (h c) -> p h c", c=640)[:, :, 0:128], 0.0))
            ctx.op("vector", lambda e: e.memset(b["v_ext"][:, 0:1024], 0.0))
            ctx.wait("tensor", "ev")
            ctx.wait("sync", "ev")
        ctx.op("vector", lambda e: e.tensor_copy(b["ctx_sb"][:, 0:512], b["ctx_imp"][:, 0:512]))
        ctx.op("vector", lambda e: e.tensor_copy(b["zsb"][:, 0:4], b["z_imp"][:, 0:4]))
        ctx.op("vector", lambda e: e.tensor_scalar_add(b["z4"][:, 0:4], b["z_imp"][:, 0:4], 1.0e-6))
        ctx.op("vector", lambda e: e.reciprocal(b["rz"][:, 0:4], b["z4"][:, 0:4]))
        ctx.wait("tensor", "ev")

        lnT_free = [0, 0]
        r_free = [0, 0]
        for c in range(nch):
            lnT = b["lnT"][c % 2]
            ctx.wait("sync", "ct", lnT_free[c % 2])
            for k in range(KD):
                ctx.dma_tr(lnT[:, ts(k, 512)], b["xq_dram"].ap()[c * 512:(c + 1) * 512, ts(k, 128)])
            ctx.wait("tensor", "sd")

            for m in range(4):
                ps = psb[m % 2]
                for k in range(KD):
                    ctx.op("tensor", lambda e, k=k, m=m, ps=ps, lnT=lnT: e.matmul(
                        ps[:, 0:512], b["Wq"][:, k * DIM + m * 128:k * DIM + (m + 1) * 128],
                        lnT[:, ts(k, 512)], start=(k == 0), stop=(k == KD - 1)), inc=("ct" if k == KD - 1 else None))
                ctx.wait("vector", "ct")
                ctx.op("vector", lambda e, m=m, ps=ps: e.tensor_scalar(
                    b["qT"][:, ts(m, 512)], ps[:, 0:512], b["bqc"][:, m:m + 1], QSCALE, ALU.add, ALU.mult))
                ctx.wait("tensor", "ev")
                ps = psb[(m + 1) % 2]
                for k in range(KD):
                    ctx.op("tensor", lambda e, k=k, m=m, ps=ps, lnT=lnT: e.matmul(
                        ps[:, 0:512], b["Wk"][:, k * DIM + m * 128:k * DIM + (m + 1) * 128],
                        lnT[:, ts(k, 512)], start=(k == 0), stop=(k == KD - 1)), inc=("ct" if k == KD - 1 else None))
                ctx.wait("vector", "ct")
                ctx.op("vector", lambda e, m=m, ps=ps: e.tensor_scalar_add(
                    b["kT"][:, m * 640 + 128:m * 640 + 640], ps[:, 0:512], b["bkc"][:, m:m + 1]))
                ctx.wait("tensor", "ev")
            for mt in range(4):
                ps = psb[mt % 2]
                for k in range(KD):
                    ctx.op("tensor", lambda e, k=k, mt=mt, ps=ps, lnT=lnT: e.matmul(
                        ps[:, 0:512], lnT[:, k * 512 + mt * 128:k * 512 + (mt + 1) * 128],
                        b["Wq"][:, k * DIM + 512:k * DIM + 1024], start=(k == 0), stop=False))
                ctx.op("tensor", lambda e, ps=ps: e.matmul(
                    ps[:, 0:512], b["ones1"][:1, 0:128], b["bqr"][:1, 512:1024],
                    start=False, stop=True), inc="ct")
                ctx.wait("vector", "ct")
                ctx.wait("scalar", "ct")
                for h in range(4):
                    sm = b["smx"]
                    ctx.wait("vector", "es")
                    ctx.op("vector", lambda e, h=h, ps=ps: e.reduce_max(
                        sm[:, 0:1], ps[:, h * 128:(h + 1) * 128], axis=AX.X, negate=True))
                    ctx.wait("scalar", "ev")
                    ctx.op("scalar", lambda e, h=h, ps=ps: e.activation(
                        b["expf"][:, 0:128], ps[:, h * 128:(h + 1) * 128], AF.Exp, bias=sm[:, 0:1], scale=1.0))
                    ctx.wait("vector", "es")
                    ctx.op("vector", lambda e: e.reduce_sum(sm[:, 1:2], b["expf"][:, 0:128], axis=AX.X))
                    ctx.op("vector", lambda e: e.reciprocal(sm[:, 2:3], sm[:, 1:2]))
                    ctx.op("vector", lambda e, h=h: e.tensor_scalar_mul(
                        b["qsm_n"][:, ts(h, 128)], b["expf"][:, 0:128], sm[:, 2:3]))
                    ctx.wait("tensor", "ev")
                    ctx.op("tensor", lambda e, h=h: e.transpose(
                        b["psbf"][:, 0:128], b["qsm_n"][:, ts(h, 128)], b["identb"][:, :]), inc="ct")
                    ctx.wait("vector", "ct")
                    ctx.op("vector", lambda e, h=h, mt=mt: e.tensor_copy(
                        b["qsmT"][:, h * 512 + mt * 128:h * 512 + (mt + 1) * 128], b["psbf"][:, 0:128]))
                    ctx.wait("tensor", "ev")
                ps = psb[(mt + 1) % 2]
                for k in range(KD):
                    ctx.op("tensor", lambda e, k=k, mt=mt, ps=ps, lnT=lnT: e.matmul(
                        ps[:, 0:512], lnT[:, k * 512 + mt * 128:k * 512 + (mt + 1) * 128],
                        b["Wk"][:, k * DIM + 512:k * DIM + 1024], start=(k == 0), stop=False))
                ctx.op("tensor", lambda e, ps=ps: e.matmul(
                    ps[:, 0:512], b["ones1"][:1, 0:128], b["bkr"][:1, 512:1024],
                    start=False, stop=True), inc="ct")
                ctx.wait("vector", "ct")
                ctx.op("vector", lambda e, ps=ps: e.tensor_copy(b["kraw"][:, 0:512], ps[:, 0:512]))
                ctx.op("vector", lambda e: e.scalar_tensor_tensor(
                    b["kelu"][:, 0:512], b["kraw"][:, 0:512], 0.0, b["kraw"][:, 0:512], ALU.max, ALU.subtract))
                ctx.wait("scalar", "ev")
                ctx.op("scalar", lambda e: e.activation(b["kexp"][:, 0:512], b["kelu"][:, 0:512], AF.Exp, bias=0.0, scale=-1.0))
                ctx.wait("vector", "es")
                ctx.op("vector", lambda e, mt=mt: e.scalar_tensor_tensor(
                    b["k1n"][:, ts(mt, 512)], b["kraw"][:, 0:512], 0.0, b["kexp"][:, 0:512], ALU.max, ALU.add))
                ctx.wait("tensor", "ev")
                for n_ in range(2):
                    ps = psb[n_ % 2]
                    for k in range(KD):
                        ctx.op("tensor", lambda e, k=k, mt=mt, n_=n_, ps=ps, lnT=lnT: e.matmul(
                            ps[:, 0:512], lnT[:, k * 512 + mt * 128:k * 512 + (mt + 1) * 128],
                            b["Wv"][:, k * DIM + n_ * 512:k * DIM + (n_ + 1) * 512], start=(k == 0), stop=False))
                    ctx.op("tensor", lambda e, n_=n_, ps=ps: e.matmul(
                        ps[:, 0:512], b["ones1"][:1, 0:128], b["bvr"][:1, ts(n_, 512)],
                        start=False, stop=True), inc="ct")
                    ctx.wait("vector", "ct")
                    ctx.op("vector", lambda e, mt=mt, n_=n_, ps=ps: e.tensor_copy(
                        b["v_ext"][:, (1 + mt) * 1024 + n_ * 512:(1 + mt) * 1024 + (n_ + 1) * 512], ps[:, 0:512]))
                    ctx.wait("tensor", "ev")

            for h in range(4):
                for w in range(4):
                    ps = psb[w % 2]
                    ctx.op("tensor", lambda e, h=h, w=w, ps=ps: e.matmul(
                        ps[:, 0:256], b["qT"][:, h * 512 + w * 128:h * 512 + (w + 1) * 128],
                        b["kT"][:, h * 640 + w * 128:h * 640 + w * 128 + 256],
                        start=True, stop=True), inc="ct")
                    mk = b["maskF"] if (c == 0 and w == 0) else b["maskS"]
                    ctx.wait("vector", "ct")
                    ctx.wait("vector", "es")
                    ctx.op("vector", lambda e, mk=mk, ps=ps: e.tensor_tensor(
                        b["dts"][:, 0:256], ps[:, 0:256], mk[:, 0:256], ALU.add))
                    ctx.wait("tensor", "ev")
                    ctx.op("vector", lambda e: e.reduce_max(b["smx"][:, 0:1], b["dts"][:, 0:256], axis=AX.X, negate=True))
                    ctx.wait("scalar", "ev")
                    ctx.op("scalar", lambda e: e.activation(b["dex"][:, 0:256], b["dts"][:, 0:256], AF.Exp, bias=b["smx"][:, 0:1], scale=1.0))
                    ctx.wait("vector", "es")
                    ctx.op("vector", lambda e: e.reduce_sum(b["smx"][:, 1:2], b["dex"][:, 0:256], axis=AX.X))
                    ctx.op("vector", lambda e: e.reciprocal(b["smx"][:, 2:3], b["smx"][:, 1:2]))
                    ctx.op("vector", lambda e: e.tensor_scalar_mul(b["attb"][:, 0:256], b["dex"][:, 0:256], b["smx"][:, 2:3]))
                    ctx.wait("tensor", "ev")
                    for j in range(2):
                        ctx.op("tensor", lambda e, j=j: e.transpose(
                            b["psbf"][:, ts(j, 128)], b["attb"][:, ts(j, 128)], b["identb"][:, :]),
                            inc=("ct" if j == 1 else None))
                    ctx.wait("vector", "ct")
                    ctx.op("vector", lambda e: e.tensor_copy(b["attT"][:, 0:256], b["psbf"][:, 0:256]))
                    ctx.wait("tensor", "ev")
                    po = b["psf_t2"]
                    for j in range(2):
                        ctx.op("tensor", lambda e, h=h, w=w, j=j, po=po: e.matmul(
                            po[:, 0:128], b["v_ext"][:, (w + j) * 1024 + h * 128:(w + j) * 1024 + (h + 1) * 128],
                            b["attT"][:, ts(j, 128)], start=(j == 0), stop=(j == 1)), inc=("ct" if j == 1 else None))
                    ctx.wait("vector", "ct")
                    ctx.op("vector", lambda e, h=h, w=w, po=po: e.tensor_copy(
                        b["outT"][:, h * 512 + w * 128:h * 512 + (w + 1) * 128], po[:, 0:128]))
                    ctx.wait("tensor", "ev")

            for h in range(4):
                for mt in range(4):
                    po = b["psf_t2"]
                    for u in range(2):
                        row0 = 64 * u
                        pc_ = state_ps[u]
                        pz_ = state_ps[2 + u]
                        ctx.op("vector", lambda e, h=h, mt=mt, u=u: e.tensor_scalar_mul(
                            b["qz"][:, 0:64],
                            b["qsmT"][:, h * 512 + mt * 128 + u * 64:h * 512 + mt * 128 + (u + 1) * 64],
                            b["rz"][:, h:h + 1]))
                        ctx.wait("tensor", "ev")
                        ctx.op("tensor", lambda e, h=h, u=u, po=po: e.matmul(
                            po[:, 128 + u * 64:128 + (u + 1) * 64], b["ctx_sb"][:, ts(h, 128)],
                            b["qz"][:, 0:64], start=True, stop=True))
                        ctx.op("tensor", lambda e, h=h, mt=mt, row0=row0, pc_=pc_: e.matmul(
                            pc_[:, 0:128],
                            b["k1n"][row0:row0 + 64, mt * 512 + h * 128:mt * 512 + (h + 1) * 128],
                            b["v_ext"][row0:row0 + 64, (1 + mt) * 1024 + 512 + h * 128:(1 + mt) * 1024 + 512 + (h + 1) * 128],
                            start=True, stop=True))
                        ctx.op("tensor", lambda e, h=h, mt=mt, row0=row0, pz_=pz_: e.matmul(
                            pz_[:, 0:1],
                            b["k1n"][row0:row0 + 64, mt * 512 + h * 128:mt * 512 + (h + 1) * 128],
                            b["onescol"][row0:row0 + 64, 0:1], start=True, stop=True), inc="ct")
                        ctx.wait("vector", "ct")
                        ctx.op("vector", lambda e, h=h, pc_=pc_: e.tensor_tensor(
                            b["ctx_sb"][:, ts(h, 128)], b["ctx_sb"][:, ts(h, 128)], pc_[:, 0:128], ALU.add))
                        ctx.op("vector", lambda e, h=h, pz_=pz_: e.tensor_tensor(
                            b["zsb"][:, h:h + 1], b["zsb"][:, h:h + 1], pz_[:, 0:1], ALU.add))
                        ctx.op("vector", lambda e, h=h: e.tensor_scalar_add(
                            b["z4"][:, h:h + 1], b["zsb"][:, h:h + 1], 1.0e-6))
                        ctx.op("vector", lambda e, h=h: e.reciprocal(b["rz"][:, h:h + 1], b["z4"][:, h:h + 1]))
                        ctx.wait("tensor", "ev")
                    ctx.wait("vector", "ct")
                    ctx.op("vector", lambda e, h=h, mt=mt, po=po: e.tensor_copy(
                        b["outT"][:, (4 + h) * 512 + mt * 128:(4 + h) * 512 + (mt + 1) * 128], po[:, 128:256]))
                    ctx.wait("tensor", "ev")

            for mt in range(4):
                rt = b["rti"][mt % 2]
                ctx.wait("sync", "ev", r_free[mt % 2])
                ctx.dma("sync", rt[:, :], b["r_dram"].ap()[c * 512 + mt * 128:c * 512 + (mt + 1) * 128, :])
                for n_ in range(2):
                    ps = psb[n_ % 2]
                    for k in range(KD):
                        ctx.op("tensor", lambda e, k=k, mt=mt, n_=n_, ps=ps: e.matmul(
                            ps[:, 0:512], b["outT"][:, k * 512 + mt * 128:k * 512 + (mt + 1) * 128],
                            b["Wo"][:, k * DIM + n_ * 512:k * DIM + (n_ + 1) * 512], start=(k == 0), stop=False))
                    ctx.op("tensor", lambda e, n_=n_, L=L, ps=ps: e.matmul(
                        ps[:, 0:512], b["ones1"][:1, 0:128], b["abor"][:1, ts(n_, 512)],
                        start=False, stop=True), inc="ct")
                    ctx.wait("vector", "ct")
                    ctx.wait("vector", "sd")
                    ctx.op("vector", lambda e, n_=n_, rt=rt, ps=ps: e.tensor_tensor(
                        rt[:, ts(n_, 512)], rt[:, ts(n_, 512)], ps[:, 0:512], ALU.add))
                    ctx.wait("tensor", "ev")
                ctx.wait("sync", "ev")
                ctx.dma("sync", b["r_dram"].ap()[c * 512 + mt * 128:c * 512 + (mt + 1) * 128, :], rt[:, :])
                r_free[mt % 2] = ctx.ckpt("ev")
            ctx.op("vector", lambda e: e.tensor_copy(
                b["kT"][:, :].rearrange("p (h c) -> p h c", c=640)[:, :, 0:128],
                b["kT"][:, :].rearrange("p (h c) -> p h c", c=640)[:, :, 512:640]))
            ctx.op("vector", lambda e: e.tensor_copy(b["v_ext"][:, 0:1024], b["v_ext"][:, 4096:5120]))
            ctx.wait("tensor", "ev")
            lnT_free[c % 2] = ctx.cnt["ct"]
        ctx.misc.setdefault("_wfree", [0])[0] = ctx.cnt["ct"]

        if export:
            ctx.wait("sync", "ev")
            ctx.dma("sync", b["st_state"].ap()[ts(L, 128), 0:512], b["ctx_sb"][:, 0:512])
            ctx.dma("sync", b["st_state"].ap()[ts(L, 128), 512:516], b["zsb"][:, 0:4])
            ctx.dma("sync", b["st_khalo"].ap()[ts(L, 128), :].rearrange("p (h c) -> p h c", c=128),
                    b["kT"][:, :].rearrange("p (h c) -> p h c", c=640)[:, :, 0:128])
            ctx.dma("sync", b["st_vhalo"].ap()[ts(L, 128), :], b["v_ext"][:, 0:1024])
            ctx.wait("gpsimd", "sd")
            ctx.wait("gpsimd", "sg")
            ctx.op("gpsimd", lambda e, L=L: e.memset(b["flag_sb"][:1, :], MAGIC + L + 1))
            ctx.wait("gpsimd", "eg")
            ctx.dma("gpsimd", b["sflag"].ap()[:1, :], b["flag_sb"][:1, :])
            exp_free = ctx.ckpt("sd")

        _emit_ffn(ctx, n_tok, L)


def _emit_ffn(ctx, n_tok, L):
    b = ctx.b
    psb = b["psb"]
    nch = n_tok // 512
    _emit_ln_pass(ctx, n_tok, b["xq_dram"])
    for half in range(2):
        w1h, w2h = b["W1h"], b["W2h"]
        ctx.wait("gpsimd", "ct", ctx.misc.setdefault("_wfree", [0])[0])
        ctx.wait("gpsimd", "sd")
        ctx.wait("gpsimd", "ev")
        ctx.wait("gpsimd", "es")
        ctx.dma("gpsimd", w1h[:, :].rearrange("p (a c) -> p a c", c=2048),
                ctx.inp["L0_attn_W1"][L, :, half * 2048:(half + 1) * 2048].rearrange("(a p) c -> p a c", p=128))
        ctx.dma("gpsimd", w2h[:, :].rearrange("p (a c) -> p a c", c=DIM),
                ctx.inp["L0_attn_W2"][L, half * 2048:(half + 1) * 2048, :].rearrange("(a p) c -> p a c", p=128))
        if half == 0:
            ctx.wait("sync", "ct", ctx.misc.setdefault("_wfree", [0])[0])
            ctx.dma("sync", b["ab2r"][:1, :], ctx.inp["m_ab2r"][:, ts(L, DIM)])
        ctx.wait("tensor", "sg")
        ctx.wait("tensor", "sd")
        for n4 in range(4):
            ps = psb[n4 % 2]
            for k in range(KD):
                ctx.op("tensor", lambda e, k=k, n4=n4, ps=ps, L=L: e.matmul(
                    ps[:1, 0:512], b["be2c"][:, L * 8 + k:L * 8 + k + 1],
                    w1h[:, k * 2048 + n4 * 512:k * 2048 + (n4 + 1) * 512],
                    start=(k == 0), stop=(k == KD - 1)), inc=("ct" if k == KD - 1 else None))
            ctx.wait("vector", "ct")
            ctx.op("vector", lambda e, n4=n4, ps=ps: e.tensor_copy(b["b2w1r"][:1, ts(n4, 512)], ps[:1, 0:512]))
            ctx.wait("tensor", "ev")
        for m in range(16):
            ps = psb[m % 2]
            ctx.op("tensor", lambda e, m=m, ps=ps: e.matmul(
                ps[:, 0:1], b["b2w1r"][:1, ts(m, 128)], b["ones1"][:1, 0:1],
                start=True, stop=True), inc="ct")
            ctx.wait("vector", "ct")
            ctx.op("vector", lambda e, m=m, ps=ps, L=L, half=half: e.tensor_tensor(
                b["bffc"][:, m:m + 1], ps[:, 0:1], b["ab1c"][:, L * 32 + half * 16 + m:L * 32 + half * 16 + m + 1], ALU.add))
            ctx.wait("tensor", "ev")
        for k in range(KD):
            ctx.op("vector", lambda e, k=k, L=L: e.tensor_scalar_mul(
                w1h[:, ts(k, 2048)], w1h[:, ts(k, 2048)], b["g2c"][:, L * 8 + k:L * 8 + k + 1]))
        ctx.wait("tensor", "ev")
        ctx.wait("gpsimd", "ev")

        lnT_free = [0, 0]
        r_free = [0, 0]
        ctx.wait("sync", "sd")
        for c in range(nch):
            lnT = b["lnT"][c % 2]
            ctx.wait("sync", "ct", lnT_free[c % 2])
            for k in range(KD):
                ctx.dma_tr(lnT[:, ts(k, 512)], b["xq_dram"].ap()[c * 512:(c + 1) * 512, ts(k, 128)])
            ctx.wait("tensor", "sd")
            ctx.wait("scalar", "sd")
            for m in range(16):
                ps = psb[m % 2]
                for k in range(KD):
                    ctx.op("tensor", lambda e, k=k, m=m, ps=ps, lnT=lnT: e.matmul(
                        ps[:, 0:512], w1h[:, k * 2048 + m * 128:k * 2048 + (m + 1) * 128],
                        lnT[:, ts(k, 512)], start=(k == 0), stop=(k == KD - 1)), inc=("ct" if k == KD - 1 else None))
                ctx.wait("scalar", "ct")
                ctx.op("scalar", lambda e, m=m, ps=ps: e.activation(
                    b["hT"][:, ts(m, 512)], ps[:, 0:512], AF.Gelu, bias=b["bffc"][:, m:m + 1], scale=1.0))
                ctx.wait("tensor", "es")
            for mt in range(4):
                rt = b["rti"][mt % 2]
                ctx.wait("sync", "ev", r_free[mt % 2])
                ctx.dma("sync", rt[:, :], b["r_dram"].ap()[c * 512 + mt * 128:c * 512 + (mt + 1) * 128, :])
                for n_ in range(2):
                    ps = psb[n_ % 2]
                    for k in range(16):
                        last = (k == 15) and (half == 1)
                        ctx.op("tensor", lambda e, k=k, mt=mt, n_=n_, ps=ps, last=last: e.matmul(
                            ps[:, 0:512], b["hT"][:, k * 512 + mt * 128:k * 512 + (mt + 1) * 128],
                            w2h[:, k * DIM + n_ * 512:k * DIM + (n_ + 1) * 512], start=(k == 0), stop=last),
                            inc=("ct" if last else None))
                    if half == 0:
                        ctx.op("tensor", lambda e, n_=n_, L=L, ps=ps: e.matmul(
                            ps[:, 0:512], b["ones1"][:1, 0:128], b["ab2r"][:1, ts(n_, 512)],
                            start=False, stop=True), inc="ct")
                    ctx.wait("vector", "ct")
                    ctx.wait("vector", "sd")
                    ctx.op("vector", lambda e, n_=n_, rt=rt, ps=ps: e.tensor_tensor(
                        rt[:, ts(n_, 512)], rt[:, ts(n_, 512)], ps[:, 0:512], ALU.add))
                    ctx.wait("tensor", "ev")
                ctx.wait("sync", "ev")
                ctx.dma("sync", b["r_dram"].ap()[c * 512 + mt * 128:c * 512 + (mt + 1) * 128, :], rt[:, :])
                r_free[mt % 2] = ctx.ckpt("ev")
            lnT_free[c % 2] = ctx.cnt["ct"]
        ctx.misc.setdefault("_wfree", [0])[0] = ctx.cnt["ct"]


def build_nc(test_mode=None):
    nc = bass.Bass("TRN2", target_bir_lowering=False)

    inp = {}
    inp["x_in"] = nc.dram_tensor("x_in", [T0, DIM], F32, kind="ExternalInput").ap()
    for nm, shp in [
        ("L0_ffn_W1", [E8, HID, DIM]), ("L0_ffn_W2", [E8, DIM, HID]), ("L0_ffn_W3", [E8, HID, DIM]),
        ("L0_ffn_A1", [E8, RANK, DIM]), ("L0_ffn_A3", [E8, RANK, DIM]),
        ("L0_attn_Wq", [DEPTH, DIM, DIM]), ("L0_attn_Wk", [DEPTH, DIM, DIM]),
        ("L0_attn_Wv", [DEPTH, DIM, DIM]), ("L0_attn_Wo", [DEPTH, DIM, DIM]),
        ("L0_attn_W1", [DEPTH, DIM, FF]), ("L0_attn_W2", [DEPTH, FF, DIM]),
    ]:
        inp[nm] = nc.dram_tensor(nm, shp, F32, kind="ExternalInput").ap()
    for nm, shp, dt in [
        ("m_b1c", [128, E8 * 16], F32), ("m_b3c", [128, E8 * 16], F32),
        ("m_b2", [8, DIM], F32), ("m_wgT", [128, E8 * 8], BF16),
        ("m_B1t", [16, E8 * HID], BF16), ("m_B3t", [16, E8 * HID], BF16),
        ("m_B2t", [16, E8 * DIM], BF16), ("m_A2t", [128, E8 * 256], BF16),
        ("identb", [128, 128], BF16), ("identf", [128, 128], F32),
        ("m_ln1g", [128, DEPTH * 8], F32), ("m_ln1b", [128, DEPTH * 8], BF16),
        ("m_ln2g", [128, DEPTH * 8], F32), ("m_ln2b", [128, DEPTH * 8], BF16),
        ("m_ab1c", [128, DEPTH * 32], F32),
        ("m_ab2r", [1, DEPTH * DIM], F32), ("m_abor", [1, DEPTH * DIM], F32),
        ("m_maskS", [128, 256], BF16), ("m_maskF", [128, 256], BF16),
        ("m_cfg", [1, 8], I32),
    ]:
        inp[nm] = nc.dram_tensor(nm, shp, dt, kind="ExternalInput").ap()
    inp["out"] = nc.dram_tensor("out", [T0, DIM], F32, kind="ExternalOutput").ap()

    dram = {
        "xq_dram": nc.dram_tensor("xq_dram", [T0, DIM], BF16),
        "r_dram": nc.dram_tensor("r_dram", [T0, DIM], F32),
        "xnT_dram": nc.dram_tensor("xnT_dram", [DIM, T0], BF16),
        "y_dram": nc.dram_tensor("y_dram", [T0, DIM], F32),
        "terd": nc.dram_tensor("terd", [256, T0], BF16),
        "w1b_dram": nc.dram_tensor("w1b_dram", [HID, DIM], BF16),
        "w3b_dram": nc.dram_tensor("w3b_dram", [HID, DIM], BF16),
        "w2b_dram": nc.dram_tensor("w2b_dram", [DIM, HID], BF16),
        "gatesT_dram": nc.dram_tensor("gatesT_dram", [8, T0], F32),
        "g_sh": nc.dram_tensor("g_sh", [T0, DIM], F32, addr_space="Shared"),
        "st_state": nc.dram_tensor("st_state", [DEPTH * 128, 516], F32, addr_space="Shared"),
        "st_khalo": nc.dram_tensor("st_khalo", [DEPTH * 128, 512], BF16, addr_space="Shared"),
        "st_vhalo": nc.dram_tensor("st_vhalo", [DEPTH * 128, 1024], BF16, addr_space="Shared"),
        "sflag": nc.dram_tensor("sflag", [1, 8], I32, addr_space="Shared"),
        "st_state_p": nc.dram_tensor("st_state_p", [DEPTH * 128, 516], F32),
        "st_khalo_p": nc.dram_tensor("st_khalo_p", [DEPTH * 128, 512], BF16),
        "st_vhalo_p": nc.dram_tensor("st_vhalo_p", [DEPTH * 128, 1024], BF16),
        "sflag_p": nc.dram_tensor("sflag_p", [1, 8], I32),
        "gflag": nc.dram_tensor("gflag", [1, 8], I32, addr_space="Shared"),
        "gflag_priv": nc.dram_tensor("gflag_priv", [1, 8], I32),
    }

    with ExitStack() as st:
        E = st.enter_context
        sems = {k: E(nc.semaphore(k)) for k in ("sd", "sg", "sq", "ct", "ev", "es", "eg", "sp", "ip")}

        BFA = E(nc.sbuf_tensor("BFA", [128, 75776], BF16))
        F32A = E(nc.sbuf_tensor("F32A", [128, 11520], F32))
        FLG = E(nc.sbuf_tensor("FLG", [1, 8], I32))
        FLG2 = E(nc.sbuf_tensor("FLG2", [1, 8], I32))

        def ps(name, shape, dt, d):
            d[name] = E(nc.psum_tensor(name, shape, dt))

        pbufs = {}
        ps("psb0", [128, 512], F32, pbufs)
        ps("psb1", [128, 512], F32, pbufs)
        ps("psbf", [128, 512], BF16, pbufs)
        ps("psf_t2", [128, 512], F32, pbufs)
        ps("p0", [128, 512], F32, pbufs)
        ps("p1", [128, 512], F32, pbufs)
        ps("p2", [128, 512], F32, pbufs)
        ps("p3", [128, 512], F32, pbufs)

        class Carver:
            def __init__(self, arena, total):
                self.a = arena
                self.off = 0
                self.total = total

            def take(self, name, cols, d, rows=128):
                assert self.off + cols <= self.total, (name, self.off, cols, self.total)
                v = self.a[0:rows, self.off:self.off + cols]
                d[name] = v
                self.off += cols
                return v

        def moe_views():
            d = dict(dram)
            d.update(pbufs)
            d["psb"] = [pbufs["psb0"], pbufs["psb1"]]
            d["flag_sb"] = FLG
            bc = Carver(BFA, 75776)
            bc.take("w1T", KD * HID, d)
            bc.take("w3T", KD * HID, d)
            bc.take("w2T", 16 * DIM, d)
            gs = bc.take("gsb", 16 * 512, d)
            bc.take("wstage", 2048, d)
            xn0 = bc.take("xnS0", 4096, d); xn1 = bc.take("xnS1", 4096, d)
            d["xnS"] = [xn0, xn1]
            bc.take("tmp1", 512, d)
            bc.take("tmp3", 512, d)
            b1te = bc.take("B1Te", 2048, d)
            bc.take("B3Te", 2048, d)
            bc.take("B2Te", 1024, d)
            bc.take("A2Te", 256, d)
            bc.take("wgT", 64, d)
            d["ter_sb"] = b1te[:, 0:1024]
            bc.take("terAe", 512, d)
            bc.take("terBe", 512, d)
            bc.take("ter2", 512, d)
            bc.take("identb", 128, d)
            d["a13"] = gs[:, 0:2048]
            d["A13T"] = gs[:, 2048:4096]
            d["lnxb"] = [gs[:, 4096:5120], gs[:, 5120:6144]]
            fc = Carver(F32A, 11520)
            ys = fc.take("ysub", 4096, d)
            go = fc.take("gout", 2048, d)
            d["lnx"] = [ys[:, 0:1024], ys[:, 1024:2048]]
            d["lnxf"] = [ys[:, 2048:3072], ys[:, 3072:4096]]
            d["lnsq"] = go[:, 0:1024]
            fc.take("gates", 352, d)
            fc.take("gT_sb", 512, d, rows=8)
            fc.take("b2nat", 1024, d, rows=8)
            fc.take("b1c", 128, d)
            fc.take("b3c", 128, d)
            fc.take("zero1k", 1024, d)
            fc.take("identf", 128, d)
            fc.take("lg", 8, d)
            fc.take("g_m1n", 1, d); fc.take("g_m1p", 1, d)
            fc.take("g_m2p", 1, d); fc.take("g_sc", 1, d)
            fc.take("g_eq", 8, d); fc.take("g_l2", 8, d); fc.take("g_ind", 8, d)
            fc.take("g_ex", 8, d)
            fc.take("ln_s1", 1, d); fc.take("ln_s2", 1, d)
            fc.take("ln_mm", 1, d); fc.take("ln_rs", 1, d)
            d["xnTt"] = [BFA[:, 0:T0], BFA[:, 0:T0]]
            d["flag2_sb"] = FLG2
            return d

        bufs_moe = moe_views()

        def attn_views():
            d = dict(dram)
            d.update(pbufs)
            d["psb"] = [pbufs["psb0"], pbufs["psb1"]]
            d["flag_sb"] = FLG
            d["flag2_sb"] = FLG2
            bc = Carver(BFA, 75776)
            bc.take("Wq", KD * DIM, d)
            bc.take("Wk", KD * DIM, d)
            bc.take("Wv", KD * DIM, d)
            bc.take("Wo", KD * DIM, d)
            d["W1h"] = BFA[:, 0:16384]
            d["W2h"] = BFA[:, 16384:32768]
            ln0 = bc.take("lnT0", KD * 512, d)
            ln1 = bc.take("lnT1", KD * 512, d)
            d["lnT"] = [ln0, ln1]
            h0 = bc.off
            bc.take("kT", 4 * 640, d)
            bc.take("qT", 4 * 512, d)
            bc.take("qsmT", 4 * 512, d)
            bc.take("k1n", 4 * 512, d)
            d["hT"] = BFA[:, h0:h0 + 16 * 512]
            bc.take("v_ext", 5 * 1024, d)
            bc.take("outT", 8 * 512, d)
            bc.take("attb", 256, d)
            bc.take("attT", 256, d)
            bc.take("qsm_n", 512, d)
            bc.take("identb", 128, d)
            bc.take("onescol", 1, d)
            bc.take("alxb0", 1024, d); bc.take("alxb1", 1024, d)
            d["alxb"] = [d["alxb0"], d["alxb1"]]
            bc.take("maskS", 256, d)
            bc.take("maskF", 256, d)
            bc.take("be1c", DEPTH * 8, d)
            bc.take("be2c", DEPTH * 8, d)

            fc = Carver(F32A, 11520)
            r0 = fc.take("rti0", 1024, d); r1 = fc.take("rti1", 1024, d)
            d["rti"] = [r0, r1]
            d["alx"] = [r0, r1]
            als = fc.take("alsq", 1024, d)
            d["kelu"] = als[:, 0:512]
            d["kexp"] = als[:, 512:1024]
            ctx_off = fc.off
            cs_ = fc.take("ctx_sb", 512, d)
            ci_ = fc.take("ctx_imp", 512, d)
            kro = fc.off
            fc.take("dts", 256, d)
            fc.take("dex", 256, d)
            d["kraw"] = F32A[:, kro:kro + 512]
            fc.take("expf", 128, d)
            fc.take("qz", 64, d)
            fc.take("g1c", DEPTH * 8, d)
            fc.take("g2c", DEPTH * 8, d)
            fc.take("ab1c", DEPTH * 32, d)
            bro = fc.take("abor", DIM, d, rows=1)
            d["ab2r"] = bro
            bq_off = fc.off
            fc.take("bqr", DIM, d, rows=1)
            fc.take("bkr", DIM, d, rows=1)
            fc.take("bvr", DIM, d, rows=1)
            d["b2w1r"] = F32A[0:1, bq_off:bq_off + 2048]
            fc.take("bqc", 8, d); fc.take("bkc", 8, d)
            fc.take("bffc", 16, d)
            fc.take("smx", 4, d)
            fc.take("z4", 4, d); fc.take("rz", 4, d); fc.take("zsb", 4, d)
            fc.take("z_imp", 4, d); fc.take("zex", 4, d)
            fc.take("ones1", 128, d, rows=1)
            fc.take("ln_s1", 1, d); fc.take("ln_s2", 1, d)
            fc.take("ln_mm", 1, d); fc.take("ln_rs", 1, d)
            cx0 = fc.take("cmb_x0", 1024, d); cx1 = fc.take("cmb_x1", 1024, d)
            d["cmb_x"] = [cx0, cx1]
            d["identf"] = cx0[:, 0:128]
            d["zero1k"] = cx0
            d["cmb_r"] = [r0, r1]
            d["cmb_g"] = [als, F32A[:, ctx_off:ctx_off + 1024]]
            d["_wfree"] = [0]
            d["lnx"] = [r0, r1]
            d["lnxf"] = [cx0, cx1]
            d["lnxb"] = d["alxb"]
            d["lnsq"] = als
            return d

        bufs_attn = attn_views()

        block = E(nc.Block())

        def run_moe_test(ctx, ntk=T0):
            ctx.dma("sync", ctx.b["identb"][:, :], ctx.inp["identb"][:, :])
            ctx.dma("sync", ctx.b["identf"][:, :], ctx.inp["identf"][:, :])
            ctx.wait("tensor", "sd")
            emit_xn(ctx, ntk, want_r=False, want_xnT=True)
            emit_moe(ctx, ntk, ctx.inp["out"], ctx.b["gflag_priv"].ap())

        def run_cd_even(ctx):
            ctx.dma("sync", ctx.b["identb"][:, :], ctx.inp["identb"][:, :])
            ctx.dma("sync", ctx.b["identf"][:, :], ctx.inp["identf"][:, :])
            ctx.wait("tensor", "sd")
            emit_xn(ctx, T0, want_r=True, want_xnT=False)
            emit_attn(ctx, T0, import_states=False, export=True)
            ctx.wait("sync", "sd")
            ctx.wait("sync", "ev")
            ctx.dma("sync", ctx.inp["out"][:, :], ctx.b["r_dram"].ap()[:, :])
            ctx.wait("sync", "sd")

        def run_cd_odd(ctx):
            ctx.dma("sync", ctx.b["identb"][:, :], ctx.inp["identb"][:, :])
            ctx.dma("sync", ctx.b["identf"][:, :], ctx.inp["identf"][:, :])
            ctx.wait("tensor", "sd")
            emit_xn(ctx, T1, want_r=True, want_xnT=bool(CD_WITH_MOE))
            if CD_WITH_MOE in (1, 6):
                emit_moe(ctx, T1, ctx.b["g_sh"].ap(), ctx.b["gflag"].ap())
            if CD_WITH_MOE not in (4, 6):
                emit_attn(ctx, T1, import_states=(CD_WITH_MOE != 3), export=False,
                          nlayers=(1 if CD_WITH_MOE == 5 else DEPTH))
            ctx.wait("sync", "sd")
            ctx.wait("sync", "ev")
            ctx.dma("sync", ctx.inp["out"][:, :], ctx.b["r_dram"].ap()[:, :])
            ctx.wait("sync", "sd")

        def run_attn_test(ctx):
            ctx.b["st_state"] = ctx.b["st_state_p"]
            ctx.b["st_khalo"] = ctx.b["st_khalo_p"]
            ctx.b["st_vhalo"] = ctx.b["st_vhalo_p"]
            ctx.b["sflag"] = ctx.b["sflag_p"]
            ctx.dma("sync", ctx.b["identb"][:, :], ctx.inp["identb"][:, :])
            ctx.wait("tensor", "sd")
            emit_xn(ctx, T0, want_r=True, want_xnT=False)
            emit_attn(ctx, T0, import_states=False)
            ctx.wait("sync", "sd")
            ctx.wait("sync", "ev")
            ctx.dma("sync", ctx.inp["out"][:, :], ctx.b["r_dram"].ap()[:, :])
            ctx.wait("sync", "sd")

        def run_role_A(ctx):
            ctx.dma("sync", ctx.b["identb"][:, :], ctx.inp["identb"][:, :])
            ctx.dma("sync", ctx.b["identf"][:, :], ctx.inp["identf"][:, :])
            ctx.wait("tensor", "sd")
            emit_xn(ctx, T0, want_r=True, want_xnT=False)
            emit_attn(ctx, T0, import_states=False)
            emit_poll(ctx, ctx.b["gflag"].ap(), T0 // 512, 200)
            emit_combine(ctx, T0, ctx.b["g_sh"].ap())

        def run_role_B(ctx):
            ctx.dma("sync", ctx.b["identb"][:, :], ctx.inp["identb"][:, :])
            ctx.dma("sync", ctx.b["identf"][:, :], ctx.inp["identf"][:, :])
            ctx.wait("tensor", "sd")
            emit_zero_out(ctx, 0, T0)
            emit_xn(ctx, T0, want_r=False, want_xnT=True)
            emit_moe(ctx, T0, ctx.b["g_sh"].ap(), ctx.b["gflag"].ap())

        def run_role_C(ctx):
            ctx.dma("sync", ctx.b["identb"][:, :], ctx.inp["identb"][:, :])
            ctx.dma("sync", ctx.b["identf"][:, :], ctx.inp["identf"][:, :])
            ctx.wait("tensor", "sd")
            emit_zero_out(ctx, 0, T0)
            emit_xn(ctx, T0, want_r=True, want_xnT=False)
            emit_attn(ctx, T0, import_states=False)

        def run_role_D(ctx):
            ctx.dma("sync", ctx.b["identb"][:, :], ctx.inp["identb"][:, :])
            ctx.dma("sync", ctx.b["identf"][:, :], ctx.inp["identf"][:, :])
            ctx.wait("tensor", "sd")
            emit_zero_out(ctx, T1, T0)
            emit_xn(ctx, T1, want_r=True, want_xnT=True)
            emit_moe(ctx, T1, ctx.b["g_sh"].ap(), ctx.b["gflag"].ap())
            emit_attn(ctx, T1, import_states=True, export=False)
            emit_poll(ctx, ctx.b["gflag"].ap(), T1 // 512, 201)
            emit_combine(ctx, T1, ctx.b["g_sh"].ap())
            if DEBUG_DUMP:
                ctx.wait("sync", "sq")
                ctx.dma("sync", ctx.inp["out"][T1:2 * T1, :], ctx.b["r_dram"].ap()[0:T1, :])
                ctx.dma("sync", ctx.inp["out"][2 * T1:2 * T1 + 512, :], ctx.b["g_sh"].ap()[0:512, :])
                ctx.wait("sync", "sd")

        roles = [(run_role_A, bufs_attn), (run_role_B, bufs_moe),
                 (run_role_C, bufs_attn), (run_role_D, None)]

        for ename in ENGINES:
            def engine_body(eng, ename=ename):
                if test_mode == "moe":
                    ctx = Ctx(ename, eng, sems, dict(bufs_moe), inp)
                    run_moe_test(ctx)
                    return
                if test_mode == "moe_t1":
                    ctx = Ctx(ename, eng, sems, dict(bufs_moe), inp)
                    run_moe_test(ctx, ntk=T1)
                    return
                if test_mode == "attn":
                    ctx = Ctx(ename, eng, sems, dict(bufs_attn), inp)
                    if ename == "gpsimd":
                        with eng.register("pollreg") as pr:
                            eng.reg_mov(pr, 0)
                            ctx.b["pollreg"] = pr
                            run_attn_test(ctx)
                    else:
                        run_attn_test(ctx)
                    return
                if test_mode == "cd":
                    with eng.register("role") as rr, eng.register("pollreg") as pr:
                        eng.reg_mov(rr, eng.partition_id())
                        eng.reg_alu(rr, rr, 1, ALU.bitwise_and)
                        eng.reg_mov(pr, 0)
                        with eng.If_eq(rr, 0):
                            ctx = Ctx(ename, eng, sems, dict(bufs_attn), inp)
                            ctx.b["pollreg"] = pr
                            run_cd_even(ctx)
                        with eng.If_eq(rr, 1):
                            bvx = dict(bufs_moe)
                            bvx.update({k: v for k, v in bufs_attn.items()
                                        if k not in ("lnx", "lnxf", "lnxb", "lnsq",
                                                     "ln_s1", "ln_s2", "ln_mm", "ln_rs",
                                                     "zero1k", "flag_sb", "flag2_sb",
                                                     "identb", "identf")})
                            ctx = Ctx(ename, eng, sems, bvx, inp)
                            ctx.b["pollreg"] = pr
                            run_cd_odd(ctx)
                    return
                with eng.register("role") as rr, eng.register("pollreg") as pr:
                    eng.reg_mov(rr, eng.partition_id())
                    eng.reg_alu(rr, rr, 3, ALU.bitwise_and)
                    eng.reg_mov(pr, 0)
                    for ridx, (fn, bv) in enumerate(roles):
                        if bv is None:
                            bv = dict(bufs_moe)
                            bv.update({k: v for k, v in bufs_attn.items()
                                       if k not in ("lnx", "lnxf", "lnxb", "lnsq",
                                                    "ln_s1", "ln_s2", "ln_mm", "ln_rs",
                                                    "zero1k", "flag_sb", "flag2_sb",
                                                    "identb", "identf")})
                        with eng.If_eq(rr, ridx):
                            ctx = Ctx(ename, eng, sems, dict(bv), inp)
                            ctx.b["pollreg"] = pr
                            fn(ctx)
            getattr(block, ename)(engine_body)

    return nc


def _prep_inputs(i):
    bf = ml_dtypes.bfloat16
    f32 = np.float32
    d = {}
    d["m_b1c"] = np.ascontiguousarray(np.asarray(i["L0_ffn_b1"], f32).reshape(E8 * 16, 128).T)
    d["m_b3c"] = np.ascontiguousarray(np.asarray(i["L0_ffn_b3"], f32).reshape(E8 * 16, 128).T)
    d["m_b2"] = np.ascontiguousarray(np.asarray(i["L0_ffn_b2"], f32))
    wgt = np.asarray(i["L0_ffn_Wg"], f32).T.reshape(KD, 128, E8).transpose(1, 0, 2).reshape(128, KD * E8)
    d["m_wgT"] = np.ascontiguousarray(wgt).astype(bf)
    d["m_B1t"] = _bt(i["L0_ffn_B1"])
    d["m_B3t"] = _bt(i["L0_ffn_B3"])
    d["m_B2t"] = _bt(i["L0_ffn_B2"])
    a2 = np.asarray(i["L0_ffn_A2"], f32).transpose(0, 2, 1).reshape(E8, 16, 128, RANK)
    d["m_A2t"] = np.ascontiguousarray(a2.transpose(2, 0, 1, 3).reshape(128, E8 * 256)).astype(bf)
    d["identb"] = np.eye(128, dtype=f32).astype(bf)
    d["identf"] = np.eye(128, dtype=f32)

    def cols(v, per):
        v = np.asarray(v, f32).reshape(DEPTH, per, 128)
        return np.ascontiguousarray(v.transpose(2, 0, 1).reshape(128, DEPTH * per))

    d["m_ln1g"] = cols(i["L0_attn_ln1_g"], 8)
    d["m_ln1b"] = cols(i["L0_attn_ln1_b"], 8).astype(bf)
    d["m_ln2g"] = cols(i["L0_attn_ln2_g"], 8)
    d["m_ln2b"] = cols(i["L0_attn_ln2_b"], 8).astype(bf)
    d["m_ab1c"] = cols(i["L0_attn_b1"], 32)
    d["m_ab2r"] = np.ascontiguousarray(np.asarray(i["L0_attn_b2"], f32).reshape(1, DEPTH * DIM))
    d["m_abor"] = np.ascontiguousarray(np.asarray(i["L0_attn_bo"], f32).reshape(1, DEPTH * DIM))
    ii = np.arange(128)[:, None]
    jj = np.arange(256)[None, :]
    causal = np.where((jj >= 128) & (ii < (jj - 128)), NEG, 0.0).astype(f32)
    maskS = causal.copy()
    maskF = causal.copy()
    maskF[:, 0:128] = NEG
    d["m_maskS"] = maskS.astype(bf)
    d["m_maskF"] = maskF.astype(bf)
    d["m_cfg"] = np.zeros((1, 8), np.int32)
    return d


def _bt(B):
    B = np.asarray(B, np.float32)
    e, rows, r = B.shape
    out = np.ascontiguousarray(B.transpose(2, 0, 1).reshape(r, e * rows))
    return out.astype(ml_dtypes.bfloat16)


def kernel(**inputs):
    nc = build_nc(TEST_MODE)
    prep = _prep_inputs(inputs)
    x = np.asarray(inputs["L0_attn_x"])
    wnames = ("L0_ffn_W1", "L0_ffn_W2", "L0_ffn_W3", "L0_ffn_A1", "L0_ffn_A3",
              "L0_attn_Wq", "L0_attn_Wk", "L0_attn_Wv", "L0_attn_Wo",
              "L0_attn_W1", "L0_attn_W2")
    ws = {nm: np.ascontiguousarray(np.asarray(inputs[nm], np.float32)) for nm in wnames}
    in_maps = []
    for c in range(8):
        m = dict(prep)
        m.update(ws)
        role = c % 4
        bb = c // 4
        if role == 3 and TEST_MODE is None:
            xs = np.zeros((T0, DIM), np.float32)
            xs[0:T1] = x[bb, T0:T_, :]
            m["x_in"] = xs
            m["m_maskF"] = prep["m_maskS"]
        else:
            m["x_in"] = np.ascontiguousarray(x[bb, 0:T0, :], np.float32)
        in_maps.append(m)
    res = run_bass_kernel_spmd(nc, in_maps, core_ids=list(range(8)))
    out = np.zeros((B_, T_, DIM), np.float32)
    if TEST_MODE is None:
        for bb in range(B_):
            out[bb, 0:T0] = res.results[4 * bb]["out"]
            out[bb, T0:T_] = res.results[4 * bb + 3]["out"][0:T1]
    else:
        for bb in range(B_):
            out[bb, 0:T0] = res.results[4 * bb]["out"]
    return out


# version 04
# speedup vs baseline: 2.7963x; 2.7963x over previous
import numpy as np
import ml_dtypes
from contextlib import ExitStack

import concourse.bass as bass
import concourse.mybir as mybir
from concourse.bass_utils import run_bass_kernel_spmd

F32 = mybir.dt.float32
BF16 = mybir.dt.bfloat16
I32 = mybir.dt.int32
AF = mybir.ActivationFunctionType
ALU = mybir.AluOpType
AX = mybir.AxisListType

B_, T_, DIM = 2, 8192, 1024
DEPTH, HEADS, DH = 4, 8, 128
NLOCAL, WINDOW, BUCKET = 4, 128, 64
FF = 4096
E8, HID, RANK = 8, 2048, 16
T0 = 5632
T1 = T_ - T0
KD = DIM // 128
QSCALE = float(DH) ** -0.5
NEG = -1.0e30
EPS = 1.0e-5

ENGINES = ("tensor", "vector", "scalar", "gpsimd", "sync")

TEST_MODE = None
DEBUG_DUMP = False
CD_WITH_MOE = False


def ts(i, n):
    return slice(i * n, (i + 1) * n)


class Ctx:
    DMA_SEMS = ("sd", "sg", "sq")

    def __init__(self, engine_name, eng, sems, bufs, inp):
        self.engine = engine_name
        self.e = eng
        self.sems = sems
        self.cnt = {k: 0 for k in sems}
        self.hw = {k: 0 for k in sems}
        self.guard = {k: 0 for k in sems}
        self.b = bufs
        self.inp = inp
        self.misc = {}

    def wait(self, engine, sem, val=None):
        if val is None:
            val = self.cnt[sem]
        if val <= 0:
            return
        self.hw[sem] = max(self.hw[sem], val)
        if self.engine == engine:
            self.e.wait_ge(self.sems[sem], val)

    def bump(self, sem, n=1):
        self.cnt[sem] += n
        return self.cnt[sem]

    SELF_SEM = {"vector": "ev", "scalar": "es", "gpsimd": "eg"}

    def op(self, engine, fn, inc=None, inc_n=1):
        s = self.SELF_SEM.get(engine)
        if s is not None:
            if self.engine == engine and self.cnt[s] > 0:
                self.e.wait_ge(self.sems[s], self.cnt[s])
            if inc is None:
                inc, inc_n = s, 1
        inst = None
        if self.engine == engine:
            inst = fn(self.e)
        if inc is not None:
            self.bump(inc, inc_n)
            if inst is not None:
                inst.then_inc(self.sems[inc], inc_n)
        return inst

    def ckpt(self, sem):
        self.hw[sem] = max(self.hw[sem], self.cnt[sem])
        return self.cnt[sem]

    def _dma_guard(self, queue, sem):
        if self.hw[sem] > self.guard[sem]:
            tgt = self.hw[sem]
            if self.engine == queue:
                self.e.wait_ge(self.sems[sem], tgt)
            self.guard[sem] = tgt

    def dma(self, queue, out, in_, sem=None, **kw):
        if sem is None:
            sem = {"sync": "sd", "gpsimd": "sg", "scalar": "sq"}[queue]
        self._dma_guard(queue, sem)
        return self.op(queue, lambda e: e.dma_start(out=out, in_=in_, **kw),
                       inc=sem, inc_n=16)

    def dma_tr(self, out, in_, sem="sd"):
        self._dma_guard("sync", sem)
        return self.op("sync", lambda e: e.dma_start_transpose(out, in_),
                       inc=sem, inc_n=16)


def emit_xn(ctx, n_tok, want_r, want_xnT):
    b = ctx.b
    x_in = ctx.inp["x_in"]
    ntt = n_tok // 128
    free = [0, 0]
    for i in range(ntt):
        pb = i % 2
        xt, xf, xb16 = b["lnx"][pb], b["lnxf"][pb], b["lnxb"][pb]
        ctx.wait("sync", "ev", free[pb])
        ctx.dma("sync", xt[:, :], x_in[ts(i, 128), :])
        ctx.wait("vector", "sd")
        s1, s2, mm_, rs, xsq = b["ln_s1"], b["ln_s2"], b["ln_mm"], b["ln_rs"], b["lnsq"]
        ctx.op("vector", lambda e: e.tensor_mul(xsq[:, :], xt[:, :], xt[:, :]))
        ctx.op("vector", lambda e: e.reduce_sum(s1[:, :], xt[:, :], axis=AX.X))
        ctx.op("vector", lambda e: e.reduce_sum(s2[:, :], xsq[:, :], axis=AX.X))
        ctx.op("vector", lambda e: e.tensor_scalar_mul(s1[:, :], s1[:, :], 1.0 / DIM))
        ctx.op("vector", lambda e: e.tensor_mul(mm_[:, :], s1[:, :], s1[:, :]))
        ctx.op("vector", lambda e: e.tensor_scalar(s2[:, :], s2[:, :], 1.0 / DIM, EPS, ALU.mult, ALU.add))
        ctx.op("vector", lambda e: e.tensor_sub(s2[:, :], s2[:, :], mm_[:, :]))
        ctx.op("vector", lambda e: e.reciprocal(s2[:, :], s2[:, :]), inc="ev")
        ctx.wait("scalar", "ev")
        ctx.op("scalar", lambda e: e.activation(rs[:, :], s2[:, :], AF.Sqrt), inc="es")
        ctx.wait("vector", "es")
        ctx.op("vector", lambda e: e.tensor_scalar(xf[:, :], xt[:, :], s1[:, :1], rs[:, :1], ALU.subtract, ALU.mult))
        ctx.op("vector", lambda e: e.tensor_copy(xb16[:, :], xf[:, :]), inc="ev")
        free[pb] = ctx.cnt["ev"]
        ctx.wait("sync", "ev")
        ctx.dma("sync", b["xq_dram"].ap()[ts(i, 128), :], xb16[:, :])
        if want_r:
            ctx.dma("sync", b["r_dram"].ap()[ts(i, 128), :], xf[:, :])
    if want_xnT:
        ctx.wait("sync", "sd")
        tfree = [0, 0]
        for k in range(KD):
            tb = b["xnTt"][k % 2]
            ctx.wait("sync", "sd")
            ctx.dma_tr(tb[:, :n_tok], b["xq_dram"].ap()[:n_tok, ts(k, 128)])
            ctx.wait("sync", "sd")
            ctx.dma("sync", b["xnT_dram"].ap()[ts(k, 128), :n_tok], tb[:, :n_tok])
            tfree[k % 2] = ctx.cnt["sd"]
        ctx.wait("sync", "sd")


def emit_zero_out(ctx, row0, row1):
    b = ctx.b
    z = b["zero1k"]
    ctx.op("gpsimd", lambda e: e.memset(z[:, :], 0.0), inc="eg")
    ctx.wait("gpsimd", "eg")
    for i in range(row0 // 128, row1 // 128):
        ctx.dma("gpsimd", ctx.inp["out"][ts(i, 128), :], z[:, :])


def emit_moe(ctx, n_tok, g_dst, gflag_dst):
    b = ctx.b
    nsub = n_tok // 512
    xnT = b["xnT_dram"]
    w1T, w3T, w2T = b["w1T"], b["w3T"], b["w2T"]
    psb = b["psb"]

    ctx.dma("sync", b["b1c"][:, :], ctx.inp["m_b1c"][:, :])
    ctx.dma("sync", b["b3c"][:, :], ctx.inp["m_b3c"][:, :])
    ctx.dma("sync", b["b2nat"][:8, :], ctx.inp["m_b2"][:, :])
    ctx.dma("sync", b["wgT"][:, :], ctx.inp["m_wgT"][:, :])
    ctx.dma("gpsimd", b["a13"][:, 0:1024], ctx.inp["L0_ffn_A1"].rearrange("e r d -> (e r) d"))
    ctx.dma("gpsimd", b["a13"][:, 1024:2048], ctx.inp["L0_ffn_A3"].rearrange("e r d -> (e r) d"))
    ctx.wait("tensor", "sd")
    ctx.wait("tensor", "sg")
    for k in range(KD):
        ctx.op("tensor", lambda e, k=k: e.transpose(b["psbf"][:, 0:128], b["a13"][:, ts(k, 128)], b["identb"][:, :]))
        ctx.op("tensor", lambda e, k=k: e.transpose(b["psbf"][:, 128:256], b["a13"][:, 1024 + k * 128:1024 + (k + 1) * 128], b["identb"][:, :]), inc="ct")
        ctx.wait("vector", "ct")
        ctx.op("vector", lambda e, k=k: e.tensor_copy(b["A13T"][:, ts(k, 256)], b["psbf"][:, 0:256]))
        ctx.wait("tensor", "ev")

    xnS_free = [0, 0]
    for s in range(nsub):
        xnS = b["xnS"][s % 2]
        ctx.wait("sync", "ct", xnS_free[s % 2])
        for k in range(KD):
            ctx.dma("sync", xnS[:, ts(k, 512)], xnT.ap()[ts(k, 128), ts(s, 512)])
        ctx.wait("tensor", "sd")
        for mt in range(4):
            i = s * 4 + mt
            ps = psb[mt % 2]
            for k in range(KD):
                ctx.op("tensor", lambda e, k=k, mt=mt, ps=ps, xnS=xnS: e.matmul(
                    ps[:, 0:8], xnS[:, k * 512 + mt * 128:k * 512 + (mt + 1) * 128],
                    b["wgT"][:, ts(k, 8)], start=(k == 0), stop=(k == KD - 1)),
                    inc=("ct" if k == KD - 1 else None))
            ctx.wait("vector", "ct")
            lg = b["lg"]
            m1n, m1p, m2p, gsc = b["g_m1n"], b["g_m1p"], b["g_m2p"], b["g_sc"]
            ctx.op("vector", lambda e, ps=ps: e.tensor_copy(lg[:, :], ps[:, 0:8]))
            ctx.wait("tensor", "ev")
            ctx.op("vector", lambda e: e.reduce_max(m1n[:, :], lg[:, :], axis=AX.X, negate=True))
            ctx.op("vector", lambda e: e.tensor_scalar_mul(m1p[:, :], m1n[:, :], -1.0))
            ctx.op("vector", lambda e: e.tensor_single_scalar(b["g_eq"][:, :], lg[:, :], m1p[:, :1], ALU.is_equal))
            ctx.op("vector", lambda e: e.scalar_tensor_tensor(b["g_l2"][:, :], b["g_eq"][:, :], NEG, lg[:, :], ALU.mult, ALU.add))
            ctx.op("vector", lambda e: e.reduce_max(m2p[:, :], b["g_l2"][:, :], axis=AX.X))
            ctx.op("vector", lambda e: e.tensor_single_scalar(b["g_ind"][:, :], lg[:, :], m2p[:, :1], ALU.is_ge))
            ctx.wait("scalar", "ev")
            ctx.op("scalar", lambda e: e.activation(b["g_ex"][:, :], lg[:, :], AF.Exp, bias=m1n[:, :1], scale=1.0))
            ctx.wait("vector", "es")
            ctx.op("vector", lambda e: e.tensor_mul(b["g_ex"][:, :], b["g_ex"][:, :], b["g_ind"][:, :]))
            ctx.op("vector", lambda e: e.reduce_sum(gsc[:, :], b["g_ex"][:, :], axis=AX.X))
            ctx.op("vector", lambda e: e.reciprocal(gsc[:, :], gsc[:, :]))
            ctx.op("vector", lambda e, i=i: e.tensor_scalar_mul(b["gates"][:, ts(i, 8)], b["g_ex"][:, :], gsc[:, :1]))
            ctx.wait("tensor", "ev")
            ctx.op("tensor", lambda e, i=i: e.transpose(b["psf_t2"][:8, 0:128], b["gates"][:, ts(i, 8)], b["identf"][:, :]), inc="ct")
            ctx.wait("vector", "ct")
            ctx.op("vector", lambda e, i=i, mt=mt: e.tensor_copy(b["gT_sb"][:8, ts(mt, 128)], b["psf_t2"][:8, 0:128]))
            ctx.wait("tensor", "ev")
        ctx.wait("sync", "ev")
        ctx.dma("sync", b["gatesT_dram"].ap()[:, ts(s, 512)], b["gT_sb"][:8, 0:512])
        for m in range(2):
            ps = psb[m % 2]
            for k in range(KD):
                ctx.op("tensor", lambda e, k=k, m=m, ps=ps, xnS=xnS: e.matmul(
                    ps[:, 0:512], b["A13T"][:, k * 256 + m * 128:k * 256 + (m + 1) * 128],
                    xnS[:, ts(k, 512)], start=(k == 0), stop=(k == KD - 1)),
                    inc=("ct" if k == KD - 1 else None))
            ctx.wait("vector", "ct")
            ctx.op("vector", lambda e, m=m, ps=ps: e.tensor_copy(b["ter_sb"][:, ts(m, 512)], ps[:, 0:512]))
            ctx.wait("tensor", "ev")
        ctx.wait("sync", "ev")
        ctx.dma("sync", b["terd"].ap()[:, ts(s, 512)].rearrange("(m p) t -> p m t", p=128),
                b["ter_sb"][:, 0:1024].rearrange("p (m t) -> p m t", t=512))
        xnS_free[s % 2] = ctx.cnt["ct"]

    wsrc = {"w1": ctx.inp["L0_ffn_W1"], "w3": ctx.inp["L0_ffn_W3"], "w2": ctx.inp["L0_ffn_W2"]}
    wdram = {"w1": b["w1b_dram"], "w3": b["w3b_dram"], "w2": b["w2b_dram"]}
    wsb = {"w1": w1T, "w3": w3T, "w2": w2T}
    stage = b["wstage"]
    stage_free = [0]
    cast_done = {}
    wT_ready = {}
    wfree = {(w, k): 0 for w in ("w1", "w3") for k in range(KD)}
    wfree.update({("w2", k): 0 for k in range(16)})
    ydone = [0] * nsub
    ysub_free = [0]
    lora_free = [0]

    def cast_weights(e_idx):
        for wname in ("w1", "w3", "w2"):
            rows = HID if wname != "w2" else DIM
            cols = DIM if wname != "w2" else HID
            qr = rows // 8
            for h in range(8):
                sb_ = stage
                ctx.wait("gpsimd", "sd", stage_free[0])
                ctx.dma("gpsimd", sb_[:, : qr * cols // 128].rearrange("p (a c) -> p a c", c=cols),
                        wsrc[wname][e_idx, h * qr:(h + 1) * qr, :].rearrange("(a p) c -> p a c", p=128))
                ctx.wait("sync", "sg")
                ctx.dma("sync", wdram[wname].ap()[h * qr:(h + 1) * qr, :].rearrange("(a p) c -> p a c", p=128),
                        sb_[:, : qr * cols // 128].rearrange("p (a c) -> p a c", c=cols))
                stage_free[0] = ctx.ckpt("sd")
        cast_done[e_idx] = ctx.ckpt("sd")

    def load_wT(e_idx):
        ctx.wait("sync", "sd", cast_done.get(e_idx, 0))
        for wname in ("w1", "w3"):
            for k in range(KD):
                ctx.wait("sync", "ct", wfree[(wname, k)])
                ctx.dma_tr(wsb[wname][:, ts(k, HID)], wdram[wname].ap()[:, ts(k, 128)])
                wT_ready[(wname, k)] = ctx.ckpt("sd")
        for j in range(16):
            ctx.wait("sync", "ct", wfree[("w2", j)])
            ctx.dma_tr(wsb["w2"][:, ts(j, DIM)], wdram["w2"].ap()[:, ts(j, 128)])
            wT_ready[("w2", j)] = ctx.ckpt("sd")

    cast_weights(0)
    load_wT(0)

    for e_idx in range(E8):
        ctx.wait("sync", "ct", lora_free[0])
        ctx.dma("sync", b["B1Te"][:16, :], ctx.inp["m_B1t"][:, ts(e_idx, HID)])
        ctx.dma("sync", b["B3Te"][:16, :], ctx.inp["m_B3t"][:, ts(e_idx, HID)])
        ctx.dma("sync", b["B2Te"][:16, :], ctx.inp["m_B2t"][:, ts(e_idx, DIM)])
        ctx.dma("sync", b["A2Te"][:, :], ctx.inp["m_A2t"][:, ts(e_idx, 256)])
        for s in range(nsub):
            xnS = b["xnS"][s % 2]
            ctx.wait("sync", "ct", xnS_free[s % 2])
            for k in range(KD):
                ctx.dma("sync", xnS[:, ts(k, 512)], xnT.ap()[ts(k, 128), ts(s, 512)])
            ctx.wait("sync", "ct", lora_free[0])
            ctx.dma("sync", b["terAe"][:16, :], b["terd"].ap()[e_idx * 16:(e_idx + 1) * 16, ts(s, 512)])
            ctx.dma("sync", b["terBe"][:16, :], b["terd"].ap()[128 + e_idx * 16:128 + (e_idx + 1) * 16, ts(s, 512)])
            ctx.wait("tensor", "sd")
            ctx.wait("tensor", "es")
            ctx.wait("tensor", "ev")
            bank_ev = {0: 0, 1: 0}
            for m in range(16):
                for which in ("h1", "h3"):
                    wT_ = w1T if which == "h1" else w3T
                    Bt = b["B1Te"] if which == "h1" else b["B3Te"]
                    tere = b["terAe"] if which == "h1" else b["terBe"]
                    bc = b["b1c"] if which == "h1" else b["b3c"]
                    tmp = b["tmp1"] if which == "h1" else b["tmp3"]
                    ps = psb[0 if which == "h1" else 1]
                    ctx.wait("tensor", "sd", wT_ready[("w1" if which == "h1" else "w3", KD - 1)])
                    bl = bank_ev[0 if which == "h1" else 1]
                    if bl:
                        ctx.wait("tensor", bl[0], bl[1])
                    for k in range(KD):
                        ctx.op("tensor", lambda e, k=k, m=m, wT_=wT_, ps=ps, xnS=xnS: e.matmul(
                            ps[:, 0:512], wT_[:, k * HID + m * 128:k * HID + (m + 1) * 128],
                            xnS[:, ts(k, 512)], start=(k == 0), stop=False))
                    ctx.op("tensor", lambda e, m=m, Bt=Bt, tere=tere, ps=ps: e.matmul(
                        ps[:, 0:512], Bt[:16, ts(m, 128)],
                        tere[:16, 0:512], start=False, stop=True), inc="ct")
                    bcol = bc[:, e_idx * 16 + m:e_idx * 16 + m + 1]
                    bank = 0 if which == "h1" else 1
                    if which == "h1":
                        ctx.wait("scalar", "ct")
                        ctx.wait("scalar", "ev", ctx.misc.get(("tmp1ev", 0), 0))
                        ctx.op("scalar", lambda e, bcol=bcol, tmp=tmp, ps=ps: e.activation(
                            tmp[:, :], ps[:, 0:512], AF.Silu, bias=bcol, scale=1.0))
                        bank_ev[0] = ("es", ctx.cnt["es"])
                    else:
                        ctx.wait("vector", "ct")
                        ctx.op("vector", lambda e, bcol=bcol, tmp=tmp, ps=ps: e.tensor_scalar_add(
                            tmp[:, :], ps[:, 0:512], bcol[:, :1]))
                        bank_ev[1] = ("ev", ctx.cnt["ev"])
                ctx.wait("vector", "es")
                ctx.op("vector", lambda e, m=m: e.tensor_mul(b["gsb"][:, ts(m, 512)], b["tmp1"][:, :], b["tmp3"][:, :]))
                ctx.misc[("tmp1ev", 0)] = ctx.cnt["ev"]
            for k in range(KD):
                wfree[("w1", k)] = ctx.cnt["ct"]
                wfree[("w3", k)] = ctx.cnt["ct"]
            ps2 = b["psf_t2"]
            ctx.wait("tensor", "ev")
            for j in range(16):
                ctx.op("tensor", lambda e, j=j: e.matmul(
                    ps2[:16, 0:512], b["A2Te"][:, ts(j, 16)],
                    b["gsb"][:, ts(j, 512)], start=(j == 0), stop=(j == 15)),
                    inc=("ct" if j == 15 else None))
            ctx.wait("vector", "ct")
            ctx.op("vector", lambda e: e.tensor_copy(b["ter2"][:16, :], ps2[:16, 0:512]))
            ctx.wait("tensor", "ev")
            ctx.wait("tensor", "sd", wT_ready[("w2", 15)])
            ybank = {0: 0, 1: 0}
            for mt in range(4):
                i = s * 4 + mt
                for n_ in range(2):
                    ps = psb[(mt * 2 + n_) % 2]
                    if ybank[(mt * 2 + n_) % 2]:
                        ctx.wait("tensor", "es", ybank[(mt * 2 + n_) % 2])
                    for k in range(16):
                        ctx.op("tensor", lambda e, k=k, mt=mt, n_=n_, ps=ps: e.matmul(
                            ps[:, 0:512], b["gsb"][:, k * 512 + mt * 128:k * 512 + (mt + 1) * 128],
                            w2T[:, k * DIM + n_ * 512:k * DIM + (n_ + 1) * 512],
                            start=(k == 0), stop=False))
                    ctx.op("tensor", lambda e, mt=mt, n_=n_, ps=ps: e.matmul(
                        ps[:, 0:512], b["ter2"][:16, ts(mt, 128)],
                        b["B2Te"][:16, ts(n_, 512)], start=False, stop=True), inc="ct")
                    ctx.wait("scalar", "ct")
                    ctx.wait("scalar", "sg", ysub_free[0])
                    gcol = b["gates"][:, i * 8 + e_idx:i * 8 + e_idx + 1]
                    ctx.op("scalar", lambda e, mt=mt, n_=n_, gcol=gcol, ps=ps: e.activation(
                        b["ysub"][:, mt * 1024 + n_ * 512:mt * 1024 + (n_ + 1) * 512],
                        ps[:, 0:512], AF.Copy, bias=0.0, scale=gcol))
                    ybank[(mt * 2 + n_) % 2] = ctx.cnt["es"]
            for k in range(16):
                wfree[("w2", k)] = ctx.cnt["ct"]
            lora_free[0] = ctx.cnt["ct"]
            xnS_free[s % 2] = ctx.cnt["ct"]
            ctx.wait("gpsimd", "es")
            ctx.wait("gpsimd", "sg", ydone[s])
            kw = {} if e_idx == 0 else {"accum_op": ALU.add}
            ctx.dma("gpsimd", b["y_dram"].ap()[ts(s, 512), :].rearrange("(a p) c -> p a c", p=128),
                    b["ysub"][:, 0:4096].rearrange("p (a c) -> p a c", c=1024), **kw)
            ydone[s] = ctx.ckpt("sg")
            ysub_free[0] = ydone[s]
        if e_idx + 1 < E8:
            cast_weights(e_idx + 1)
            load_wT(e_idx + 1)

    gship_free = 0
    ctx.ckpt("sg")
    ctx.wait("sync", "sg")
    for s in range(nsub):
        yb = b["ysub"]
        ctx.wait("sync", "es", gship_free)
        ctx.dma("sync", yb[:, 0:4096].rearrange("p (a c) -> p a c", c=1024),
                b["y_dram"].ap()[ts(s, 512), :].rearrange("(a p) c -> p a c", p=128))
        ctx.dma("sync", b["gT_sb"][:8, 0:512], b["gatesT_dram"].ap()[:, ts(s, 512)])
        ctx.wait("tensor", "sd")
        for mt in range(4):
            for n_ in range(2):
                ps = psb[(mt * 2 + n_) % 2]
                ctx.op("tensor", lambda e, mt=mt, n_=n_, ps=ps: e.matmul(
                    ps[:, 0:512], b["gT_sb"][:8, ts(mt, 128)], b["b2nat"][:8, ts(n_, 512)],
                    start=True, stop=True), inc="ct")
                ctx.wait("vector", "ct")
                ctx.op("vector", lambda e, mt=mt, n_=n_, ps=ps: e.tensor_tensor(
                    yb[:, mt * 1024 + n_ * 512:mt * 1024 + (n_ + 1) * 512],
                    yb[:, mt * 1024 + n_ * 512:mt * 1024 + (n_ + 1) * 512],
                    ps[:, 0:512], ALU.add))
                ctx.wait("tensor", "ev")
        for hh in range(2):
            ctx.wait("scalar", "ev")
            ctx.wait("scalar", "sq")
            ctx.op("scalar", lambda e, hh=hh: e.activation(b["gout"][:, 0:2048], yb[:, hh * 2048:(hh + 1) * 2048], AF.Silu))
            ctx.wait("scalar", "es")
            ctx.dma("scalar", g_dst[s * 512 + hh * 256:s * 512 + (hh + 1) * 256, :].rearrange("(a p) c -> p a c", p=128),
                    b["gout"][:, 0:2048].rearrange("p (a c) -> p a c", c=1024))
        gship_free = ctx.cnt["es"]
        ctx.wait("gpsimd", "sq")
        ctx.wait("gpsimd", "sg")
        ctx.op("gpsimd", lambda e, s=s: e.memset(b["flag_sb"][:1, :], MAGIC + s + 1))
        ctx.wait("gpsimd", "eg")
        ctx.dma("gpsimd", gflag_dst[:1, :], b["flag_sb"][:1, :])
    ctx.wait("gpsimd", "sg")
    ctx.wait("sync", "sd")
    ctx.wait("scalar", "sq")
    ctx.wait("sync", "sq")
    ctx.wait("sync", "es")
    ctx.wait("sync", "eg")
    ctx.wait("sync", "sg")
    ctx.wait("vector", "sq")
    ctx.wait("gpsimd", "sq")
    ctx.wait("scalar", "sg")



MAGIC = 0x5AFE0000


def emit_poll(ctx, flag_ap, target, widx):
    b = ctx.b
    if ctx.engine == "gpsimd":
        e = ctx.e
        with e.register(f"pfv{widx}") as fv, e.register(f"pt{widx}") as t1, \
             e.register(f"pc1{widx}") as c1, e.register(f"pc2{widx}") as c2, \
             e.register(f"pk{widx}") as ok:
            e.reg_mov(fv, 0)

            def cond():
                e.reg_alu(t1, fv, MAGIC, ALU.subtract)
                e.reg_alu(c1, t1, target, ALU.is_ge)
                e.reg_alu(c2, t1, 64, ALU.is_le)
                e.reg_alu(ok, c1, c2, ALU.bitwise_and)
                e.reg_alu(ok, ok, 1, ALU.bitwise_xor)
                return ok

            with e.While(cond):
                e.reg_add(b["pollreg"], b["pollreg"], 16)
                e.dma_start(out=b["flag2_sb"][:1, :], in_=flag_ap[:1, :]).then_inc(ctx.sems["sp"], 16)
                e.wait_ge(ctx.sems["sp"], b["pollreg"])
                e.reg_load(fv, b["flag2_sb"][:1, 0:1])
        e.sem_inc(ctx.sems["ip"], 1)
    ctx.bump("ip")


def emit_combine(ctx, n_tok, g_src):
    b = ctx.b
    ctx.wait("vector", "ip")
    ctx.wait("sync", "ip")
    ctx.wait("sync", "sd")
    free = [0, 0]
    for i in range(n_tok // 128):
        pb = i % 2
        rt, gt, xt = b["cmb_r"][pb], b["cmb_g"][pb], b["cmb_x"][pb]
        ctx.wait("sync", "sq", free[pb])
        ctx.dma("sync", rt[:, :], b["r_dram"].ap()[ts(i, 128), :])
        ctx.dma("sync", gt[:, :], g_src[ts(i, 128), :])
        ctx.dma("sync", xt[:, :], ctx.inp["x_in"][ts(i, 128), :])
        ctx.wait("vector", "sd")
        ctx.op("vector", lambda e, rt=rt, gt=gt: e.tensor_mul(rt[:, :], rt[:, :], gt[:, :]))
        ctx.op("vector", lambda e, rt=rt, xt=xt: e.tensor_add(rt[:, :], rt[:, :], xt[:, :]))
        ctx.wait("scalar", "ev")
        ctx.dma("scalar", ctx.inp["out"][ts(i, 128), :], rt[:, :])
        free[pb] = ctx.ckpt("sq")
    ctx.wait("scalar", "sq")
    ctx.wait("sync", "sq")


def _emit_ln_pass(ctx, n_tok, dst_dram):
    b = ctx.b
    ctx.wait("sync", "sd")
    free = [0, 0]
    for i in range(n_tok // 128):
        pb = i % 2
        xt, xb16 = b["alx"][pb], b["alxb"][pb]
        ctx.wait("sync", "ev", free[pb])
        ctx.dma("sync", xt[:, :], b["r_dram"].ap()[ts(i, 128), :])
        ctx.wait("vector", "sd")
        s1, s2, mm_, rs, xsq = b["ln_s1"], b["ln_s2"], b["ln_mm"], b["ln_rs"], b["alsq"]
        ctx.op("vector", lambda e, xt=xt: e.tensor_mul(xsq[:, :], xt[:, :], xt[:, :]))
        ctx.op("vector", lambda e, xt=xt: e.reduce_sum(s1[:, :], xt[:, :], axis=AX.X))
        ctx.op("vector", lambda e: e.reduce_sum(s2[:, :], xsq[:, :], axis=AX.X))
        ctx.op("vector", lambda e: e.tensor_scalar_mul(s1[:, :], s1[:, :], 1.0 / DIM))
        ctx.op("vector", lambda e: e.tensor_mul(mm_[:, :], s1[:, :], s1[:, :]))
        ctx.op("vector", lambda e: e.tensor_scalar(s2[:, :], s2[:, :], 1.0 / DIM, EPS, ALU.mult, ALU.add))
        ctx.op("vector", lambda e: e.tensor_sub(s2[:, :], s2[:, :], mm_[:, :]))
        ctx.op("vector", lambda e: e.reciprocal(s2[:, :], s2[:, :]))
        ctx.wait("scalar", "ev")
        ctx.op("scalar", lambda e: e.activation(rs[:, :], s2[:, :], AF.Sqrt))
        ctx.wait("vector", "es")
        ctx.op("vector", lambda e, xt=xt, xb16=xb16: e.tensor_scalar(
            xb16[:, :], xt[:, :], s1[:, :1], rs[:, :1], ALU.subtract, ALU.mult))
        free[pb] = ctx.cnt["ev"]
        ctx.wait("sync", "ev")
        ctx.dma("sync", dst_dram.ap()[ts(i, 128), :], xb16[:, :])
    ctx.wait("sync", "sd")


def _load_and_prep_proj(ctx, L):
    b = ctx.b
    psb = b["psb"]
    wq, wk, wv, wo = b["Wq"], b["Wk"], b["Wv"], b["Wo"]
    ctx.wait("gpsimd", "ct", ctx.misc.setdefault("_wfree", [0])[0])
    ctx.wait("gpsimd", "sd")
    ctx.wait("gpsimd", "ev")
    ctx.wait("gpsimd", "es")
    ctx.wait("gpsimd", "sq")
    for wt, nm in ((wq, "L0_attn_Wq"), (wk, "L0_attn_Wk"), (wv, "L0_attn_Wv"), (wo, "L0_attn_Wo")):
        ctx.dma("gpsimd", wt[:, :].rearrange("p (a c) -> p a c", c=DIM),
                ctx.inp[nm][L].rearrange("(a p) c -> p a c", p=128))
    ctx.wait("sync", "ct", ctx.misc.setdefault("_wfree", [0])[0])
    ctx.dma("sync", b["abor"][:1, :], ctx.inp["m_abor"][:, ts(L, DIM)])
    ctx.wait("tensor", "sg")
    ctx.wait("tensor", "sd")
    for wt, brow in ((wq, b["bqr"]), (wk, b["bkr"]), (wv, b["bvr"])):
        for n_ in range(2):
            ps = psb[n_ % 2]
            for k in range(KD):
                ctx.op("tensor", lambda e, k=k, n_=n_, wt=wt, ps=ps, L=L: e.matmul(
                    ps[:1, 0:512], b["be1c"][:, L * 8 + k:L * 8 + k + 1],
                    wt[:, k * DIM + n_ * 512:k * DIM + (n_ + 1) * 512],
                    start=(k == 0), stop=(k == KD - 1)), inc=("ct" if k == KD - 1 else None))
            ctx.wait("vector", "ct")
            ctx.op("vector", lambda e, n_=n_, brow=brow, ps=ps: e.tensor_copy(brow[:1, ts(n_, 512)], ps[:1, 0:512]))
            ctx.wait("tensor", "ev")
    for brow, bcol in ((b["bqr"], b["bqc"]), (b["bkr"], b["bkc"])):
        for k in range(4):
            ps = psb[k % 2]
            ctx.op("tensor", lambda e, k=k, brow=brow, ps=ps: e.matmul(
                ps[:, 0:1], brow[:1, ts(k, 128)], b["ones1"][:1, 0:1],
                start=True, stop=True), inc="ct")
            ctx.wait("vector", "ct")
            ctx.op("vector", lambda e, k=k, bcol=bcol, ps=ps: e.tensor_copy(bcol[:, k:k + 1], ps[:, 0:1]))
            ctx.wait("tensor", "ev")
    for wt in (wq, wk, wv):
        for k in range(KD):
            ctx.op("vector", lambda e, k=k, wt=wt, L=L: e.tensor_scalar_mul(
                wt[:, ts(k, DIM)], wt[:, ts(k, DIM)], b["g1c"][:, L * 8 + k:L * 8 + k + 1]))
    ctx.wait("tensor", "ev")
    ctx.wait("gpsimd", "ev")


def emit_attn(ctx, n_tok, import_states, export=True, nlayers=DEPTH):
    b = ctx.b
    nch = n_tok // 512
    psb = b["psb"]
    state_ps = [b["p0"], b["p1"], b["p2"], b["p3"]]

    ctx.dma("sync", b["g1c"][:, :], ctx.inp["m_ln1g"][:, :])
    ctx.dma("sync", b["be1c"][:, :], ctx.inp["m_ln1b"][:, :])
    ctx.dma("sync", b["g2c"][:, :], ctx.inp["m_ln2g"][:, :])
    ctx.dma("sync", b["be2c"][:, :], ctx.inp["m_ln2b"][:, :])
    ctx.dma("sync", b["ab1c"][:, :], ctx.inp["m_ab1c"][:, :])
    ctx.dma("sync", b["maskS"][:, :], ctx.inp["m_maskS"][:, :])
    ctx.dma("sync", b["maskF"][:, :], ctx.inp["m_maskF"][:, :])
    ctx.op("gpsimd", lambda e: e.memset(b["ones1"][:1, :], 1.0))
    ctx.op("gpsimd", lambda e: e.memset(b["onescol"][:, :], 1.0))
    ctx.wait("tensor", "sd")
    ctx.wait("tensor", "eg")
    ctx.wait("vector", "eg")
    ctx.wait("vector", "sd")
    ctx.wait("scalar", "sd")

    exp_free = 0

    for L in range(nlayers):
        _emit_ln_pass(ctx, n_tok, b["xq_dram"])
        _load_and_prep_proj(ctx, L)

        ctx.wait("vector", "ct")
        ctx.wait("sync", "ct")
        ctx.wait("sync", "ev")
        ctx.wait("vector", "sd", exp_free)
        if import_states:
            emit_poll(ctx, b["sflag"].ap(), L + 1, 100 + L)
            ctx.wait("sync", "ip")
            ctx.dma("sync", b["ctx_imp"][:, 0:512], b["st_state"].ap()[ts(L, 128), 0:512])
            ctx.dma("sync", b["z_imp"][:, 0:4], b["st_state"].ap()[ts(L, 128), 512:516])
            ctx.dma("sync", b["kT"][:, :].rearrange("p (h c) -> p h c", c=640)[:, :, 0:128],
                    b["st_khalo"].ap()[ts(L, 128), :].rearrange("p (h c) -> p h c", c=128))
            ctx.dma("sync", b["v_ext"][:, 0:1024], b["st_vhalo"].ap()[ts(L, 128), :])
            ctx.wait("vector", "sd")
            ctx.wait("tensor", "sd")
        else:
            ctx.op("vector", lambda e: e.memset(b["ctx_imp"][:, 0:512], 0.0))
            ctx.op("vector", lambda e: e.memset(b["z_imp"][:, 0:4], 0.0))
            ctx.op("vector", lambda e: e.memset(b["kT"][:, :].rearrange("p (h c) -> p h c", c=640)[:, :, 0:128], 0.0))
            ctx.op("vector", lambda e: e.memset(b["v_ext"][:, 0:1024], 0.0))
            ctx.wait("tensor", "ev")
            ctx.wait("sync", "ev")
        ctx.op("vector", lambda e: e.tensor_copy(b["ctx_sb"][:, 0:512], b["ctx_imp"][:, 0:512]))
        ctx.op("vector", lambda e: e.tensor_copy(b["zsb"][:, 0:4], b["z_imp"][:, 0:4]))
        ctx.op("vector", lambda e: e.tensor_scalar_add(b["z4"][:, 0:4], b["z_imp"][:, 0:4], 1.0e-6))
        ctx.op("vector", lambda e: e.reciprocal(b["rz"][:, 0:4], b["z4"][:, 0:4]))
        ctx.wait("tensor", "ev")

        lnT_free = [0, 0]
        r_free = [0, 0]
        for c in range(nch):
            lnT = b["lnT"][c % 2]
            ctx.wait("sync", "ct", lnT_free[c % 2])
            for k in range(KD):
                ctx.dma_tr(lnT[:, ts(k, 512)], b["xq_dram"].ap()[c * 512:(c + 1) * 512, ts(k, 128)])
            ctx.wait("tensor", "sd")

            pbank = {0: 0, 1: 0}
            for m in range(4):
                ps = psb[m % 2]
                if pbank[m % 2]:
                    ctx.wait("tensor", "ev", pbank[m % 2])
                for k in range(KD):
                    ctx.op("tensor", lambda e, k=k, m=m, ps=ps, lnT=lnT: e.matmul(
                        ps[:, 0:512], b["Wq"][:, k * DIM + m * 128:k * DIM + (m + 1) * 128],
                        lnT[:, ts(k, 512)], start=(k == 0), stop=(k == KD - 1)), inc=("ct" if k == KD - 1 else None))
                ctx.wait("vector", "ct")
                ctx.op("vector", lambda e, m=m, ps=ps: e.tensor_scalar(
                    b["qT"][:, ts(m, 512)], ps[:, 0:512], b["bqc"][:, m:m + 1], QSCALE, ALU.add, ALU.mult))
                pbank[m % 2] = ctx.cnt["ev"]
                ps = psb[(m + 1) % 2]
                if pbank[(m + 1) % 2]:
                    ctx.wait("tensor", "ev", pbank[(m + 1) % 2])
                for k in range(KD):
                    ctx.op("tensor", lambda e, k=k, m=m, ps=ps, lnT=lnT: e.matmul(
                        ps[:, 0:512], b["Wk"][:, k * DIM + m * 128:k * DIM + (m + 1) * 128],
                        lnT[:, ts(k, 512)], start=(k == 0), stop=(k == KD - 1)), inc=("ct" if k == KD - 1 else None))
                ctx.wait("vector", "ct")
                ctx.op("vector", lambda e, m=m, ps=ps: e.tensor_scalar_add(
                    b["kT"][:, m * 640 + 128:m * 640 + 640], ps[:, 0:512], b["bkc"][:, m:m + 1]))
                pbank[(m + 1) % 2] = ctx.cnt["ev"]
            for mt in range(4):
                ps = psb[mt % 2]
                if pbank[mt % 2]:
                    ctx.wait("tensor", "ev", pbank[mt % 2])
                for k in range(KD):
                    ctx.op("tensor", lambda e, k=k, mt=mt, ps=ps, lnT=lnT: e.matmul(
                        ps[:, 0:512], lnT[:, k * 512 + mt * 128:k * 512 + (mt + 1) * 128],
                        b["Wq"][:, k * DIM + 512:k * DIM + 1024], start=(k == 0), stop=False))
                ctx.op("tensor", lambda e, ps=ps: e.matmul(
                    ps[:, 0:512], b["ones1"][:1, 0:128], b["bqr"][:1, 512:1024],
                    start=False, stop=True), inc="ct")
                ctx.wait("vector", "ct")
                ctx.wait("scalar", "ct")
                for h in range(4):
                    sm = b["smx"]
                    ctx.wait("vector", "es")
                    ctx.op("vector", lambda e, h=h, ps=ps: e.reduce_max(
                        sm[:, 0:1], ps[:, h * 128:(h + 1) * 128], axis=AX.X, negate=True))
                    ctx.wait("scalar", "ev")
                    ctx.op("scalar", lambda e, h=h, ps=ps: e.activation(
                        b["expf"][:, 0:128], ps[:, h * 128:(h + 1) * 128], AF.Exp, bias=sm[:, 0:1], scale=1.0))
                    ctx.wait("vector", "es")
                    ctx.op("vector", lambda e: e.reduce_sum(sm[:, 1:2], b["expf"][:, 0:128], axis=AX.X))
                    ctx.op("vector", lambda e: e.reciprocal(sm[:, 2:3], sm[:, 1:2]))
                    ctx.op("vector", lambda e, h=h: e.tensor_scalar_mul(
                        b["qsm_n"][:, ts(h, 128)], b["expf"][:, 0:128], sm[:, 2:3]))
                    ctx.wait("tensor", "ev")
                    ctx.op("tensor", lambda e, h=h: e.transpose(
                        b["psbf"][:, 0:128], b["qsm_n"][:, ts(h, 128)], b["identb"][:, :]), inc="ct")
                    ctx.wait("vector", "ct")
                    ctx.op("vector", lambda e, h=h, mt=mt: e.tensor_copy(
                        b["qsmT"][:, h * 512 + mt * 128:h * 512 + (mt + 1) * 128], b["psbf"][:, 0:128]))
                    ctx.wait("tensor", "ev")
                pbank[mt % 2] = ctx.cnt["ev"]
                ps = psb[(mt + 1) % 2]
                if pbank[(mt + 1) % 2]:
                    ctx.wait("tensor", "ev", pbank[(mt + 1) % 2])
                for k in range(KD):
                    ctx.op("tensor", lambda e, k=k, mt=mt, ps=ps, lnT=lnT: e.matmul(
                        ps[:, 0:512], lnT[:, k * 512 + mt * 128:k * 512 + (mt + 1) * 128],
                        b["Wk"][:, k * DIM + 512:k * DIM + 1024], start=(k == 0), stop=False))
                ctx.op("tensor", lambda e, ps=ps: e.matmul(
                    ps[:, 0:512], b["ones1"][:1, 0:128], b["bkr"][:1, 512:1024],
                    start=False, stop=True), inc="ct")
                ctx.wait("vector", "ct")
                ctx.op("vector", lambda e, ps=ps: e.tensor_copy(b["kraw"][:, 0:512], ps[:, 0:512]))
                ctx.op("vector", lambda e: e.scalar_tensor_tensor(
                    b["kelu"][:, 0:512], b["kraw"][:, 0:512], 0.0, b["kraw"][:, 0:512], ALU.max, ALU.subtract))
                ctx.wait("scalar", "ev")
                ctx.op("scalar", lambda e: e.activation(b["kexp"][:, 0:512], b["kelu"][:, 0:512], AF.Exp, bias=0.0, scale=-1.0))
                ctx.wait("vector", "es")
                ctx.op("vector", lambda e, mt=mt: e.scalar_tensor_tensor(
                    b["k1n"][:, ts(mt, 512)], b["kraw"][:, 0:512], 0.0, b["kexp"][:, 0:512], ALU.max, ALU.add))
                pbank[(mt + 1) % 2] = ctx.cnt["ev"]
                for n_ in range(2):
                    ps = psb[n_ % 2]
                    if pbank[n_ % 2]:
                        ctx.wait("tensor", "ev", pbank[n_ % 2])
                    for k in range(KD):
                        ctx.op("tensor", lambda e, k=k, mt=mt, n_=n_, ps=ps, lnT=lnT: e.matmul(
                            ps[:, 0:512], lnT[:, k * 512 + mt * 128:k * 512 + (mt + 1) * 128],
                            b["Wv"][:, k * DIM + n_ * 512:k * DIM + (n_ + 1) * 512], start=(k == 0), stop=False))
                    ctx.op("tensor", lambda e, n_=n_, ps=ps: e.matmul(
                        ps[:, 0:512], b["ones1"][:1, 0:128], b["bvr"][:1, ts(n_, 512)],
                        start=False, stop=True), inc="ct")
                    ctx.wait("vector", "ct")
                    ctx.op("vector", lambda e, mt=mt, n_=n_, ps=ps: e.tensor_copy(
                        b["v_ext"][:, (1 + mt) * 1024 + n_ * 512:(1 + mt) * 1024 + (n_ + 1) * 512], ps[:, 0:512]))
                    pbank[n_ % 2] = ctx.cnt["ev"]

            ctx.wait("tensor", "ev")
            for h in range(4):
                for w in range(4):
                    ps = psb[w % 2]
                    ctx.op("tensor", lambda e, h=h, w=w, ps=ps: e.matmul(
                        ps[:, 0:256], b["qT"][:, h * 512 + w * 128:h * 512 + (w + 1) * 128],
                        b["kT"][:, h * 640 + w * 128:h * 640 + w * 128 + 256],
                        start=True, stop=True), inc="ct")
                    mk = b["maskF"] if (c == 0 and w == 0) else b["maskS"]
                    ctx.wait("vector", "ct")
                    ctx.wait("vector", "es")
                    ctx.op("vector", lambda e, mk=mk, ps=ps: e.tensor_tensor(
                        b["dts"][:, 0:256], ps[:, 0:256], mk[:, 0:256], ALU.add))
                    ctx.wait("tensor", "ev")
                    ctx.op("vector", lambda e: e.reduce_max(b["smx"][:, 0:1], b["dts"][:, 0:256], axis=AX.X, negate=True))
                    ctx.wait("scalar", "ev")
                    ctx.op("scalar", lambda e: e.activation(b["dex"][:, 0:256], b["dts"][:, 0:256], AF.Exp, bias=b["smx"][:, 0:1], scale=1.0))
                    ctx.wait("vector", "es")
                    ctx.op("vector", lambda e: e.reduce_sum(b["smx"][:, 1:2], b["dex"][:, 0:256], axis=AX.X))
                    ctx.op("vector", lambda e: e.reciprocal(b["smx"][:, 2:3], b["smx"][:, 1:2]))
                    ctx.op("vector", lambda e: e.tensor_scalar_mul(b["attb"][:, 0:256], b["dex"][:, 0:256], b["smx"][:, 2:3]))
                    ctx.wait("tensor", "ev")
                    for j in range(2):
                        ctx.op("tensor", lambda e, j=j: e.transpose(
                            b["psbf"][:, ts(j, 128)], b["attb"][:, ts(j, 128)], b["identb"][:, :]),
                            inc=("ct" if j == 1 else None))
                    ctx.wait("vector", "ct")
                    ctx.op("vector", lambda e: e.tensor_copy(b["attT"][:, 0:256], b["psbf"][:, 0:256]))
                    ctx.wait("tensor", "ev")
                    po = b["psf_t2"]
                    for j in range(2):
                        ctx.op("tensor", lambda e, h=h, w=w, j=j, po=po: e.matmul(
                            po[:, 0:128], b["v_ext"][:, (w + j) * 1024 + h * 128:(w + j) * 1024 + (h + 1) * 128],
                            b["attT"][:, ts(j, 128)], start=(j == 0), stop=(j == 1)), inc=("ct" if j == 1 else None))
                    ctx.wait("vector", "ct")
                    ctx.op("vector", lambda e, h=h, w=w, po=po: e.tensor_copy(
                        b["outT"][:, h * 512 + w * 128:h * 512 + (w + 1) * 128], po[:, 0:128]))
                    ctx.wait("tensor", "ev")

            for h in range(4):
                for mt in range(4):
                    po = b["psf_t2"]
                    for u in range(2):
                        row0 = 64 * u
                        pc_ = state_ps[u]
                        pz_ = state_ps[2 + u]
                        ctx.op("vector", lambda e, h=h, mt=mt, u=u: e.tensor_scalar_mul(
                            b["qz"][:, 0:64],
                            b["qsmT"][:, h * 512 + mt * 128 + u * 64:h * 512 + mt * 128 + (u + 1) * 64],
                            b["rz"][:, h:h + 1]))
                        ctx.wait("tensor", "ev")
                        ctx.op("tensor", lambda e, h=h, u=u, po=po: e.matmul(
                            po[:, 128 + u * 64:128 + (u + 1) * 64], b["ctx_sb"][:, ts(h, 128)],
                            b["qz"][:, 0:64], start=True, stop=True))
                        ctx.op("tensor", lambda e, h=h, mt=mt, row0=row0, pc_=pc_: e.matmul(
                            pc_[:, 0:128],
                            b["k1n"][row0:row0 + 64, mt * 512 + h * 128:mt * 512 + (h + 1) * 128],
                            b["v_ext"][row0:row0 + 64, (1 + mt) * 1024 + 512 + h * 128:(1 + mt) * 1024 + 512 + (h + 1) * 128],
                            start=True, stop=True))
                        ctx.op("tensor", lambda e, h=h, mt=mt, row0=row0, pz_=pz_: e.matmul(
                            pz_[:, 0:1],
                            b["k1n"][row0:row0 + 64, mt * 512 + h * 128:mt * 512 + (h + 1) * 128],
                            b["onescol"][row0:row0 + 64, 0:1], start=True, stop=True), inc="ct")
                        ctx.wait("vector", "ct")
                        ctx.op("vector", lambda e, h=h, pc_=pc_: e.tensor_tensor(
                            b["ctx_sb"][:, ts(h, 128)], b["ctx_sb"][:, ts(h, 128)], pc_[:, 0:128], ALU.add))
                        ctx.op("vector", lambda e, h=h, pz_=pz_: e.tensor_tensor(
                            b["zsb"][:, h:h + 1], b["zsb"][:, h:h + 1], pz_[:, 0:1], ALU.add))
                        ctx.op("vector", lambda e, h=h: e.tensor_scalar_add(
                            b["z4"][:, h:h + 1], b["zsb"][:, h:h + 1], 1.0e-6))
                        ctx.op("vector", lambda e, h=h: e.reciprocal(b["rz"][:, h:h + 1], b["z4"][:, h:h + 1]))
                        ctx.wait("tensor", "ev")
                    ctx.wait("vector", "ct")
                    ctx.op("vector", lambda e, h=h, mt=mt, po=po: e.tensor_copy(
                        b["outT"][:, (4 + h) * 512 + mt * 128:(4 + h) * 512 + (mt + 1) * 128], po[:, 128:256]))
                    ctx.wait("tensor", "ev")

            for mt in range(4):
                rt = b["rti"][mt % 2]
                ctx.wait("sync", "ev", r_free[mt % 2])
                ctx.dma("sync", rt[:, :], b["r_dram"].ap()[c * 512 + mt * 128:c * 512 + (mt + 1) * 128, :])
                for n_ in range(2):
                    ps = psb[n_ % 2]
                    for k in range(KD):
                        ctx.op("tensor", lambda e, k=k, mt=mt, n_=n_, ps=ps: e.matmul(
                            ps[:, 0:512], b["outT"][:, k * 512 + mt * 128:k * 512 + (mt + 1) * 128],
                            b["Wo"][:, k * DIM + n_ * 512:k * DIM + (n_ + 1) * 512], start=(k == 0), stop=False))
                    ctx.op("tensor", lambda e, n_=n_, L=L, ps=ps: e.matmul(
                        ps[:, 0:512], b["ones1"][:1, 0:128], b["abor"][:1, ts(n_, 512)],
                        start=False, stop=True), inc="ct")
                    ctx.wait("vector", "ct")
                    ctx.wait("vector", "sd")
                    ctx.op("vector", lambda e, n_=n_, rt=rt, ps=ps: e.tensor_tensor(
                        rt[:, ts(n_, 512)], rt[:, ts(n_, 512)], ps[:, 0:512], ALU.add))
                    ctx.wait("tensor", "ev")
                ctx.wait("sync", "ev")
                ctx.dma("sync", b["r_dram"].ap()[c * 512 + mt * 128:c * 512 + (mt + 1) * 128, :], rt[:, :])
                r_free[mt % 2] = ctx.ckpt("ev")
            ctx.op("vector", lambda e: e.tensor_copy(
                b["kT"][:, :].rearrange("p (h c) -> p h c", c=640)[:, :, 0:128],
                b["kT"][:, :].rearrange("p (h c) -> p h c", c=640)[:, :, 512:640]))
            ctx.op("vector", lambda e: e.tensor_copy(b["v_ext"][:, 0:1024], b["v_ext"][:, 4096:5120]))
            ctx.wait("tensor", "ev")
            lnT_free[c % 2] = ctx.cnt["ct"]
        ctx.misc.setdefault("_wfree", [0])[0] = ctx.cnt["ct"]

        if export:
            ctx.wait("sync", "ev")
            ctx.dma("sync", b["st_state"].ap()[ts(L, 128), 0:512], b["ctx_sb"][:, 0:512])
            ctx.dma("sync", b["st_state"].ap()[ts(L, 128), 512:516], b["zsb"][:, 0:4])
            ctx.dma("sync", b["st_khalo"].ap()[ts(L, 128), :].rearrange("p (h c) -> p h c", c=128),
                    b["kT"][:, :].rearrange("p (h c) -> p h c", c=640)[:, :, 0:128])
            ctx.dma("sync", b["st_vhalo"].ap()[ts(L, 128), :], b["v_ext"][:, 0:1024])
            ctx.wait("gpsimd", "sd")
            ctx.wait("gpsimd", "sg")
            ctx.op("gpsimd", lambda e, L=L: e.memset(b["flag_sb"][:1, :], MAGIC + L + 1))
            ctx.wait("gpsimd", "eg")
            ctx.dma("gpsimd", b["sflag"].ap()[:1, :], b["flag_sb"][:1, :])
            exp_free = ctx.ckpt("sd")

        _emit_ffn(ctx, n_tok, L)


def _emit_ffn(ctx, n_tok, L):
    b = ctx.b
    psb = b["psb"]
    nch = n_tok // 512
    _emit_ln_pass(ctx, n_tok, b["xq_dram"])
    for half in range(2):
        w1h, w2h = b["W1h"], b["W2h"]
        ctx.wait("gpsimd", "ct", ctx.misc.setdefault("_wfree", [0])[0])
        ctx.wait("gpsimd", "sd")
        ctx.wait("gpsimd", "ev")
        ctx.wait("gpsimd", "es")
        ctx.dma("gpsimd", w1h[:, :].rearrange("p (a c) -> p a c", c=2048),
                ctx.inp["L0_attn_W1"][L, :, half * 2048:(half + 1) * 2048].rearrange("(a p) c -> p a c", p=128))
        ctx.dma("gpsimd", w2h[:, :].rearrange("p (a c) -> p a c", c=DIM),
                ctx.inp["L0_attn_W2"][L, half * 2048:(half + 1) * 2048, :].rearrange("(a p) c -> p a c", p=128))
        if half == 0:
            ctx.wait("sync", "ct", ctx.misc.setdefault("_wfree", [0])[0])
            ctx.dma("sync", b["ab2r"][:1, :], ctx.inp["m_ab2r"][:, ts(L, DIM)])
        ctx.wait("tensor", "sg")
        ctx.wait("tensor", "sd")
        for n4 in range(4):
            ps = psb[n4 % 2]
            for k in range(KD):
                ctx.op("tensor", lambda e, k=k, n4=n4, ps=ps, L=L: e.matmul(
                    ps[:1, 0:512], b["be2c"][:, L * 8 + k:L * 8 + k + 1],
                    w1h[:, k * 2048 + n4 * 512:k * 2048 + (n4 + 1) * 512],
                    start=(k == 0), stop=(k == KD - 1)), inc=("ct" if k == KD - 1 else None))
            ctx.wait("vector", "ct")
            ctx.op("vector", lambda e, n4=n4, ps=ps: e.tensor_copy(b["b2w1r"][:1, ts(n4, 512)], ps[:1, 0:512]))
            ctx.wait("tensor", "ev")
        for m in range(16):
            ps = psb[m % 2]
            ctx.op("tensor", lambda e, m=m, ps=ps: e.matmul(
                ps[:, 0:1], b["b2w1r"][:1, ts(m, 128)], b["ones1"][:1, 0:1],
                start=True, stop=True), inc="ct")
            ctx.wait("vector", "ct")
            ctx.op("vector", lambda e, m=m, ps=ps, L=L, half=half: e.tensor_tensor(
                b["bffc"][:, m:m + 1], ps[:, 0:1], b["ab1c"][:, L * 32 + half * 16 + m:L * 32 + half * 16 + m + 1], ALU.add))
            ctx.wait("tensor", "ev")
        for k in range(KD):
            ctx.op("vector", lambda e, k=k, L=L: e.tensor_scalar_mul(
                w1h[:, ts(k, 2048)], w1h[:, ts(k, 2048)], b["g2c"][:, L * 8 + k:L * 8 + k + 1]))
        ctx.wait("tensor", "ev")
        ctx.wait("gpsimd", "ev")

        lnT_free = [0, 0]
        r_free = [0, 0]
        wbank = {0: 0, 1: 0}
        ctx.wait("sync", "sd")
        for c in range(nch):
            lnT = b["lnT"][c % 2]
            ctx.wait("sync", "ct", lnT_free[c % 2])
            for k in range(KD):
                ctx.dma_tr(lnT[:, ts(k, 512)], b["xq_dram"].ap()[c * 512:(c + 1) * 512, ts(k, 128)])
            ctx.wait("tensor", "sd")
            ctx.wait("scalar", "sd")
            fbank = {0: 0, 1: 0}
            for m in range(16):
                ps = psb[m % 2]
                if fbank[m % 2]:
                    ctx.wait("tensor", "es", fbank[m % 2])
                for k in range(KD):
                    ctx.op("tensor", lambda e, k=k, m=m, ps=ps, lnT=lnT: e.matmul(
                        ps[:, 0:512], w1h[:, k * 2048 + m * 128:k * 2048 + (m + 1) * 128],
                        lnT[:, ts(k, 512)], start=(k == 0), stop=(k == KD - 1)), inc=("ct" if k == KD - 1 else None))
                ctx.wait("scalar", "ct")
                ctx.op("scalar", lambda e, m=m, ps=ps: e.activation(
                    b["hT"][:, ts(m, 512)], ps[:, 0:512], AF.Gelu, bias=b["bffc"][:, m:m + 1], scale=1.0))
                fbank[m % 2] = ctx.cnt["es"]
            ctx.wait("tensor", "es")
            for mt in range(4):
                rt = b["rti"][mt % 2]
                ctx.wait("sync", "ev", r_free[mt % 2])
                ctx.dma("sync", rt[:, :], b["r_dram"].ap()[c * 512 + mt * 128:c * 512 + (mt + 1) * 128, :])
                for n_ in range(2):
                    ps = psb[n_ % 2]
                    if wbank[n_ % 2]:
                        ctx.wait("tensor", "ev", wbank[n_ % 2])
                    for k in range(16):
                        last = (k == 15) and (half == 1)
                        ctx.op("tensor", lambda e, k=k, mt=mt, n_=n_, ps=ps, last=last: e.matmul(
                            ps[:, 0:512], b["hT"][:, k * 512 + mt * 128:k * 512 + (mt + 1) * 128],
                            w2h[:, k * DIM + n_ * 512:k * DIM + (n_ + 1) * 512], start=(k == 0), stop=last),
                            inc=("ct" if last else None))
                    if half == 0:
                        ctx.op("tensor", lambda e, n_=n_, L=L, ps=ps: e.matmul(
                            ps[:, 0:512], b["ones1"][:1, 0:128], b["ab2r"][:1, ts(n_, 512)],
                            start=False, stop=True), inc="ct")
                    ctx.wait("vector", "ct")
                    ctx.wait("vector", "sd")
                    ctx.op("vector", lambda e, n_=n_, rt=rt, ps=ps: e.tensor_tensor(
                        rt[:, ts(n_, 512)], rt[:, ts(n_, 512)], ps[:, 0:512], ALU.add))
                    wbank[n_ % 2] = ctx.cnt["ev"]
                ctx.wait("sync", "ev")
                ctx.dma("sync", b["r_dram"].ap()[c * 512 + mt * 128:c * 512 + (mt + 1) * 128, :], rt[:, :])
                r_free[mt % 2] = ctx.ckpt("ev")
            lnT_free[c % 2] = ctx.cnt["ct"]
        ctx.misc.setdefault("_wfree", [0])[0] = ctx.cnt["ct"]


def build_nc(test_mode=None):
    nc = bass.Bass("TRN2", target_bir_lowering=False)

    inp = {}
    inp["x_in"] = nc.dram_tensor("x_in", [T0, DIM], F32, kind="ExternalInput").ap()
    for nm, shp in [
        ("L0_ffn_W1", [E8, HID, DIM]), ("L0_ffn_W2", [E8, DIM, HID]), ("L0_ffn_W3", [E8, HID, DIM]),
        ("L0_ffn_A1", [E8, RANK, DIM]), ("L0_ffn_A3", [E8, RANK, DIM]),
        ("L0_attn_Wq", [DEPTH, DIM, DIM]), ("L0_attn_Wk", [DEPTH, DIM, DIM]),
        ("L0_attn_Wv", [DEPTH, DIM, DIM]), ("L0_attn_Wo", [DEPTH, DIM, DIM]),
        ("L0_attn_W1", [DEPTH, DIM, FF]), ("L0_attn_W2", [DEPTH, FF, DIM]),
    ]:
        inp[nm] = nc.dram_tensor(nm, shp, F32, kind="ExternalInput").ap()
    for nm, shp, dt in [
        ("m_b1c", [128, E8 * 16], F32), ("m_b3c", [128, E8 * 16], F32),
        ("m_b2", [8, DIM], F32), ("m_wgT", [128, E8 * 8], BF16),
        ("m_B1t", [16, E8 * HID], BF16), ("m_B3t", [16, E8 * HID], BF16),
        ("m_B2t", [16, E8 * DIM], BF16), ("m_A2t", [128, E8 * 256], BF16),
        ("identb", [128, 128], BF16), ("identf", [128, 128], F32),
        ("m_ln1g", [128, DEPTH * 8], F32), ("m_ln1b", [128, DEPTH * 8], BF16),
        ("m_ln2g", [128, DEPTH * 8], F32), ("m_ln2b", [128, DEPTH * 8], BF16),
        ("m_ab1c", [128, DEPTH * 32], F32),
        ("m_ab2r", [1, DEPTH * DIM], F32), ("m_abor", [1, DEPTH * DIM], F32),
        ("m_maskS", [128, 256], BF16), ("m_maskF", [128, 256], BF16),
        ("m_cfg", [1, 8], I32),
    ]:
        inp[nm] = nc.dram_tensor(nm, shp, dt, kind="ExternalInput").ap()
    inp["out"] = nc.dram_tensor("out", [T0, DIM], F32, kind="ExternalOutput").ap()

    dram = {
        "xq_dram": nc.dram_tensor("xq_dram", [T0, DIM], BF16),
        "r_dram": nc.dram_tensor("r_dram", [T0, DIM], F32),
        "xnT_dram": nc.dram_tensor("xnT_dram", [DIM, T0], BF16),
        "y_dram": nc.dram_tensor("y_dram", [T0, DIM], F32),
        "terd": nc.dram_tensor("terd", [256, T0], BF16),
        "w1b_dram": nc.dram_tensor("w1b_dram", [HID, DIM], BF16),
        "w3b_dram": nc.dram_tensor("w3b_dram", [HID, DIM], BF16),
        "w2b_dram": nc.dram_tensor("w2b_dram", [DIM, HID], BF16),
        "gatesT_dram": nc.dram_tensor("gatesT_dram", [8, T0], F32),
        "g_sh": nc.dram_tensor("g_sh", [T0, DIM], F32, addr_space="Shared"),
        "st_state": nc.dram_tensor("st_state", [DEPTH * 128, 516], F32, addr_space="Shared"),
        "st_khalo": nc.dram_tensor("st_khalo", [DEPTH * 128, 512], BF16, addr_space="Shared"),
        "st_vhalo": nc.dram_tensor("st_vhalo", [DEPTH * 128, 1024], BF16, addr_space="Shared"),
        "sflag": nc.dram_tensor("sflag", [1, 8], I32, addr_space="Shared"),
        "st_state_p": nc.dram_tensor("st_state_p", [DEPTH * 128, 516], F32),
        "st_khalo_p": nc.dram_tensor("st_khalo_p", [DEPTH * 128, 512], BF16),
        "st_vhalo_p": nc.dram_tensor("st_vhalo_p", [DEPTH * 128, 1024], BF16),
        "sflag_p": nc.dram_tensor("sflag_p", [1, 8], I32),
        "gflag": nc.dram_tensor("gflag", [1, 8], I32, addr_space="Shared"),
        "gflag_priv": nc.dram_tensor("gflag_priv", [1, 8], I32),
    }

    with ExitStack() as st:
        E = st.enter_context
        sems = {k: E(nc.semaphore(k)) for k in ("sd", "sg", "sq", "ct", "ev", "es", "eg", "sp", "ip")}

        BFA = E(nc.sbuf_tensor("BFA", [128, 75776], BF16))
        F32A = E(nc.sbuf_tensor("F32A", [128, 11520], F32))
        FLG = E(nc.sbuf_tensor("FLG", [1, 8], I32))
        FLG2 = E(nc.sbuf_tensor("FLG2", [1, 8], I32))

        def ps(name, shape, dt, d):
            d[name] = E(nc.psum_tensor(name, shape, dt))

        pbufs = {}
        ps("psb0", [128, 512], F32, pbufs)
        ps("psb1", [128, 512], F32, pbufs)
        ps("psbf", [128, 512], BF16, pbufs)
        ps("psf_t2", [128, 512], F32, pbufs)
        ps("p0", [128, 512], F32, pbufs)
        ps("p1", [128, 512], F32, pbufs)
        ps("p2", [128, 512], F32, pbufs)
        ps("p3", [128, 512], F32, pbufs)

        class Carver:
            def __init__(self, arena, total):
                self.a = arena
                self.off = 0
                self.total = total

            def take(self, name, cols, d, rows=128):
                assert self.off + cols <= self.total, (name, self.off, cols, self.total)
                v = self.a[0:rows, self.off:self.off + cols]
                d[name] = v
                self.off += cols
                return v

        def moe_views():
            d = dict(dram)
            d.update(pbufs)
            d["psb"] = [pbufs["psb0"], pbufs["psb1"]]
            d["flag_sb"] = FLG
            bc = Carver(BFA, 75776)
            bc.take("w1T", KD * HID, d)
            bc.take("w3T", KD * HID, d)
            bc.take("w2T", 16 * DIM, d)
            gs = bc.take("gsb", 16 * 512, d)
            bc.take("wstage", 2048, d)
            xn0 = bc.take("xnS0", 4096, d); xn1 = bc.take("xnS1", 4096, d)
            d["xnS"] = [xn0, xn1]
            bc.take("tmp1", 512, d)
            bc.take("tmp3", 512, d)
            b1te = bc.take("B1Te", 2048, d)
            bc.take("B3Te", 2048, d)
            bc.take("B2Te", 1024, d)
            bc.take("A2Te", 256, d)
            bc.take("wgT", 64, d)
            d["ter_sb"] = b1te[:, 0:1024]
            bc.take("terAe", 512, d)
            bc.take("terBe", 512, d)
            bc.take("ter2", 512, d)
            bc.take("identb", 128, d)
            d["a13"] = gs[:, 0:2048]
            d["A13T"] = gs[:, 2048:4096]
            d["lnxb"] = [gs[:, 4096:5120], gs[:, 5120:6144]]
            fc = Carver(F32A, 11520)
            ys = fc.take("ysub", 4096, d)
            go = fc.take("gout", 2048, d)
            d["lnx"] = [ys[:, 0:1024], ys[:, 1024:2048]]
            d["lnxf"] = [ys[:, 2048:3072], ys[:, 3072:4096]]
            d["lnsq"] = go[:, 0:1024]
            fc.take("gates", 352, d)
            fc.take("gT_sb", 512, d, rows=8)
            fc.take("b2nat", 1024, d, rows=8)
            fc.take("b1c", 128, d)
            fc.take("b3c", 128, d)
            fc.take("zero1k", 1024, d)
            fc.take("identf", 128, d)
            fc.take("lg", 8, d)
            fc.take("g_m1n", 1, d); fc.take("g_m1p", 1, d)
            fc.take("g_m2p", 1, d); fc.take("g_sc", 1, d)
            fc.take("g_eq", 8, d); fc.take("g_l2", 8, d); fc.take("g_ind", 8, d)
            fc.take("g_ex", 8, d)
            fc.take("ln_s1", 1, d); fc.take("ln_s2", 1, d)
            fc.take("ln_mm", 1, d); fc.take("ln_rs", 1, d)
            d["xnTt"] = [BFA[:, 0:T0], BFA[:, 0:T0]]
            d["flag2_sb"] = FLG2
            return d

        bufs_moe = moe_views()

        def attn_views():
            d = dict(dram)
            d.update(pbufs)
            d["psb"] = [pbufs["psb0"], pbufs["psb1"]]
            d["flag_sb"] = FLG
            d["flag2_sb"] = FLG2
            bc = Carver(BFA, 75776)
            bc.take("Wq", KD * DIM, d)
            bc.take("Wk", KD * DIM, d)
            bc.take("Wv", KD * DIM, d)
            bc.take("Wo", KD * DIM, d)
            d["W1h"] = BFA[:, 0:16384]
            d["W2h"] = BFA[:, 16384:32768]
            ln0 = bc.take("lnT0", KD * 512, d)
            ln1 = bc.take("lnT1", KD * 512, d)
            d["lnT"] = [ln0, ln1]
            h0 = bc.off
            bc.take("kT", 4 * 640, d)
            bc.take("qT", 4 * 512, d)
            bc.take("qsmT", 4 * 512, d)
            bc.take("k1n", 4 * 512, d)
            d["hT"] = BFA[:, h0:h0 + 16 * 512]
            bc.take("v_ext", 5 * 1024, d)
            bc.take("outT", 8 * 512, d)
            bc.take("attb", 256, d)
            bc.take("attT", 256, d)
            bc.take("qsm_n", 512, d)
            bc.take("identb", 128, d)
            bc.take("onescol", 1, d)
            bc.take("alxb0", 1024, d); bc.take("alxb1", 1024, d)
            d["alxb"] = [d["alxb0"], d["alxb1"]]
            bc.take("maskS", 256, d)
            bc.take("maskF", 256, d)
            bc.take("be1c", DEPTH * 8, d)
            bc.take("be2c", DEPTH * 8, d)

            fc = Carver(F32A, 11520)
            r0 = fc.take("rti0", 1024, d); r1 = fc.take("rti1", 1024, d)
            d["rti"] = [r0, r1]
            d["alx"] = [r0, r1]
            als = fc.take("alsq", 1024, d)
            d["kelu"] = als[:, 0:512]
            d["kexp"] = als[:, 512:1024]
            ctx_off = fc.off
            cs_ = fc.take("ctx_sb", 512, d)
            ci_ = fc.take("ctx_imp", 512, d)
            kro = fc.off
            fc.take("dts", 256, d)
            fc.take("dex", 256, d)
            d["kraw"] = F32A[:, kro:kro + 512]
            fc.take("expf", 128, d)
            fc.take("qz", 64, d)
            fc.take("g1c", DEPTH * 8, d)
            fc.take("g2c", DEPTH * 8, d)
            fc.take("ab1c", DEPTH * 32, d)
            bro = fc.take("abor", DIM, d, rows=1)
            d["ab2r"] = bro
            bq_off = fc.off
            fc.take("bqr", DIM, d, rows=1)
            fc.take("bkr", DIM, d, rows=1)
            fc.take("bvr", DIM, d, rows=1)
            d["b2w1r"] = F32A[0:1, bq_off:bq_off + 2048]
            fc.take("bqc", 8, d); fc.take("bkc", 8, d)
            fc.take("bffc", 16, d)
            fc.take("smx", 4, d)
            fc.take("z4", 4, d); fc.take("rz", 4, d); fc.take("zsb", 4, d)
            fc.take("z_imp", 4, d); fc.take("zex", 4, d)
            fc.take("ones1", 128, d, rows=1)
            fc.take("ln_s1", 1, d); fc.take("ln_s2", 1, d)
            fc.take("ln_mm", 1, d); fc.take("ln_rs", 1, d)
            cx0 = fc.take("cmb_x0", 1024, d); cx1 = fc.take("cmb_x1", 1024, d)
            d["cmb_x"] = [cx0, cx1]
            d["identf"] = cx0[:, 0:128]
            d["zero1k"] = cx0
            d["cmb_r"] = [r0, r1]
            d["cmb_g"] = [als, F32A[:, ctx_off:ctx_off + 1024]]
            d["_wfree"] = [0]
            d["lnx"] = [r0, r1]
            d["lnxf"] = [cx0, cx1]
            d["lnxb"] = d["alxb"]
            d["lnsq"] = als
            return d

        bufs_attn = attn_views()

        block = E(nc.Block())

        def run_moe_test(ctx, ntk=T0):
            ctx.dma("sync", ctx.b["identb"][:, :], ctx.inp["identb"][:, :])
            ctx.dma("sync", ctx.b["identf"][:, :], ctx.inp["identf"][:, :])
            ctx.wait("tensor", "sd")
            emit_xn(ctx, ntk, want_r=False, want_xnT=True)
            emit_moe(ctx, ntk, ctx.inp["out"], ctx.b["gflag_priv"].ap())

        def run_cd_even(ctx):
            ctx.dma("sync", ctx.b["identb"][:, :], ctx.inp["identb"][:, :])
            ctx.dma("sync", ctx.b["identf"][:, :], ctx.inp["identf"][:, :])
            ctx.wait("tensor", "sd")
            emit_xn(ctx, T0, want_r=True, want_xnT=False)
            emit_attn(ctx, T0, import_states=False, export=True)
            ctx.wait("sync", "sd")
            ctx.wait("sync", "ev")
            ctx.dma("sync", ctx.inp["out"][:, :], ctx.b["r_dram"].ap()[:, :])
            ctx.wait("sync", "sd")

        def run_cd_odd(ctx):
            ctx.dma("sync", ctx.b["identb"][:, :], ctx.inp["identb"][:, :])
            ctx.dma("sync", ctx.b["identf"][:, :], ctx.inp["identf"][:, :])
            ctx.wait("tensor", "sd")
            emit_xn(ctx, T1, want_r=True, want_xnT=bool(CD_WITH_MOE))
            if CD_WITH_MOE in (1, 6):
                emit_moe(ctx, T1, ctx.b["g_sh"].ap(), ctx.b["gflag"].ap())
            if CD_WITH_MOE not in (4, 6):
                emit_attn(ctx, T1, import_states=(CD_WITH_MOE != 3), export=False,
                          nlayers=(1 if CD_WITH_MOE == 5 else DEPTH))
            ctx.wait("sync", "sd")
            ctx.wait("sync", "ev")
            ctx.dma("sync", ctx.inp["out"][:, :], ctx.b["r_dram"].ap()[:, :])
            ctx.wait("sync", "sd")

        def run_attn_test(ctx):
            ctx.b["st_state"] = ctx.b["st_state_p"]
            ctx.b["st_khalo"] = ctx.b["st_khalo_p"]
            ctx.b["st_vhalo"] = ctx.b["st_vhalo_p"]
            ctx.b["sflag"] = ctx.b["sflag_p"]
            ctx.dma("sync", ctx.b["identb"][:, :], ctx.inp["identb"][:, :])
            ctx.wait("tensor", "sd")
            emit_xn(ctx, T0, want_r=True, want_xnT=False)
            emit_attn(ctx, T0, import_states=False)
            ctx.wait("sync", "sd")
            ctx.wait("sync", "ev")
            ctx.dma("sync", ctx.inp["out"][:, :], ctx.b["r_dram"].ap()[:, :])
            ctx.wait("sync", "sd")

        def run_role_A(ctx):
            ctx.dma("sync", ctx.b["identb"][:, :], ctx.inp["identb"][:, :])
            ctx.dma("sync", ctx.b["identf"][:, :], ctx.inp["identf"][:, :])
            ctx.wait("tensor", "sd")
            emit_xn(ctx, T0, want_r=True, want_xnT=False)
            emit_attn(ctx, T0, import_states=False)
            emit_poll(ctx, ctx.b["gflag"].ap(), T0 // 512, 200)
            emit_combine(ctx, T0, ctx.b["g_sh"].ap())

        def run_role_B(ctx):
            ctx.dma("sync", ctx.b["identb"][:, :], ctx.inp["identb"][:, :])
            ctx.dma("sync", ctx.b["identf"][:, :], ctx.inp["identf"][:, :])
            ctx.wait("tensor", "sd")
            emit_zero_out(ctx, 0, T0)
            emit_xn(ctx, T0, want_r=False, want_xnT=True)
            emit_moe(ctx, T0, ctx.b["g_sh"].ap(), ctx.b["gflag"].ap())

        def run_role_C(ctx):
            ctx.dma("sync", ctx.b["identb"][:, :], ctx.inp["identb"][:, :])
            ctx.dma("sync", ctx.b["identf"][:, :], ctx.inp["identf"][:, :])
            ctx.wait("tensor", "sd")
            emit_zero_out(ctx, 0, T0)
            emit_xn(ctx, T0, want_r=True, want_xnT=False)
            emit_attn(ctx, T0, import_states=False)

        def run_role_D(ctx):
            ctx.dma("sync", ctx.b["identb"][:, :], ctx.inp["identb"][:, :])
            ctx.dma("sync", ctx.b["identf"][:, :], ctx.inp["identf"][:, :])
            ctx.wait("tensor", "sd")
            emit_zero_out(ctx, T1, T0)
            emit_xn(ctx, T1, want_r=True, want_xnT=True)
            emit_moe(ctx, T1, ctx.b["g_sh"].ap(), ctx.b["gflag"].ap())
            emit_attn(ctx, T1, import_states=True, export=False)
            emit_poll(ctx, ctx.b["gflag"].ap(), T1 // 512, 201)
            emit_combine(ctx, T1, ctx.b["g_sh"].ap())
            if DEBUG_DUMP:
                ctx.wait("sync", "sq")
                ctx.dma("sync", ctx.inp["out"][T1:2 * T1, :], ctx.b["r_dram"].ap()[0:T1, :])
                ctx.dma("sync", ctx.inp["out"][2 * T1:2 * T1 + 512, :], ctx.b["g_sh"].ap()[0:512, :])
                ctx.wait("sync", "sd")

        roles = [(run_role_A, bufs_attn), (run_role_B, bufs_moe),
                 (run_role_C, bufs_attn), (run_role_D, None)]

        for ename in ENGINES:
            def engine_body(eng, ename=ename):
                if test_mode == "moe":
                    ctx = Ctx(ename, eng, sems, dict(bufs_moe), inp)
                    run_moe_test(ctx)
                    return
                if test_mode == "moe_t1":
                    ctx = Ctx(ename, eng, sems, dict(bufs_moe), inp)
                    run_moe_test(ctx, ntk=T1)
                    return
                if test_mode == "attn":
                    ctx = Ctx(ename, eng, sems, dict(bufs_attn), inp)
                    if ename == "gpsimd":
                        with eng.register("pollreg") as pr:
                            eng.reg_mov(pr, 0)
                            ctx.b["pollreg"] = pr
                            run_attn_test(ctx)
                    else:
                        run_attn_test(ctx)
                    return
                if test_mode == "cd":
                    with eng.register("role") as rr, eng.register("pollreg") as pr:
                        eng.reg_mov(rr, eng.partition_id())
                        eng.reg_alu(rr, rr, 1, ALU.bitwise_and)
                        eng.reg_mov(pr, 0)
                        with eng.If_eq(rr, 0):
                            ctx = Ctx(ename, eng, sems, dict(bufs_attn), inp)
                            ctx.b["pollreg"] = pr
                            run_cd_even(ctx)
                        with eng.If_eq(rr, 1):
                            bvx = dict(bufs_moe)
                            bvx.update({k: v for k, v in bufs_attn.items()
                                        if k not in ("lnx", "lnxf", "lnxb", "lnsq",
                                                     "ln_s1", "ln_s2", "ln_mm", "ln_rs",
                                                     "zero1k", "flag_sb", "flag2_sb",
                                                     "identb", "identf")})
                            ctx = Ctx(ename, eng, sems, bvx, inp)
                            ctx.b["pollreg"] = pr
                            run_cd_odd(ctx)
                    return
                with eng.register("role") as rr, eng.register("pollreg") as pr:
                    eng.reg_mov(rr, eng.partition_id())
                    eng.reg_alu(rr, rr, 3, ALU.bitwise_and)
                    eng.reg_mov(pr, 0)
                    for ridx, (fn, bv) in enumerate(roles):
                        if bv is None:
                            bv = dict(bufs_moe)
                            bv.update({k: v for k, v in bufs_attn.items()
                                       if k not in ("lnx", "lnxf", "lnxb", "lnsq",
                                                    "ln_s1", "ln_s2", "ln_mm", "ln_rs",
                                                    "zero1k", "flag_sb", "flag2_sb",
                                                    "identb", "identf")})
                        with eng.If_eq(rr, ridx):
                            ctx = Ctx(ename, eng, sems, dict(bv), inp)
                            ctx.b["pollreg"] = pr
                            fn(ctx)
            getattr(block, ename)(engine_body)

    return nc


def _prep_inputs(i):
    bf = ml_dtypes.bfloat16
    f32 = np.float32
    d = {}
    d["m_b1c"] = np.ascontiguousarray(np.asarray(i["L0_ffn_b1"], f32).reshape(E8 * 16, 128).T)
    d["m_b3c"] = np.ascontiguousarray(np.asarray(i["L0_ffn_b3"], f32).reshape(E8 * 16, 128).T)
    d["m_b2"] = np.ascontiguousarray(np.asarray(i["L0_ffn_b2"], f32))
    wgt = np.asarray(i["L0_ffn_Wg"], f32).T.reshape(KD, 128, E8).transpose(1, 0, 2).reshape(128, KD * E8)
    d["m_wgT"] = np.ascontiguousarray(wgt).astype(bf)
    d["m_B1t"] = _bt(i["L0_ffn_B1"])
    d["m_B3t"] = _bt(i["L0_ffn_B3"])
    d["m_B2t"] = _bt(i["L0_ffn_B2"])
    a2 = np.asarray(i["L0_ffn_A2"], f32).transpose(0, 2, 1).reshape(E8, 16, 128, RANK)
    d["m_A2t"] = np.ascontiguousarray(a2.transpose(2, 0, 1, 3).reshape(128, E8 * 256)).astype(bf)
    d["identb"] = np.eye(128, dtype=f32).astype(bf)
    d["identf"] = np.eye(128, dtype=f32)

    def cols(v, per):
        v = np.asarray(v, f32).reshape(DEPTH, per, 128)
        return np.ascontiguousarray(v.transpose(2, 0, 1).reshape(128, DEPTH * per))

    d["m_ln1g"] = cols(i["L0_attn_ln1_g"], 8)
    d["m_ln1b"] = cols(i["L0_attn_ln1_b"], 8).astype(bf)
    d["m_ln2g"] = cols(i["L0_attn_ln2_g"], 8)
    d["m_ln2b"] = cols(i["L0_attn_ln2_b"], 8).astype(bf)
    d["m_ab1c"] = cols(i["L0_attn_b1"], 32)
    d["m_ab2r"] = np.ascontiguousarray(np.asarray(i["L0_attn_b2"], f32).reshape(1, DEPTH * DIM))
    d["m_abor"] = np.ascontiguousarray(np.asarray(i["L0_attn_bo"], f32).reshape(1, DEPTH * DIM))
    ii = np.arange(128)[:, None]
    jj = np.arange(256)[None, :]
    causal = np.where((jj >= 128) & (ii < (jj - 128)), NEG, 0.0).astype(f32)
    maskS = causal.copy()
    maskF = causal.copy()
    maskF[:, 0:128] = NEG
    d["m_maskS"] = maskS.astype(bf)
    d["m_maskF"] = maskF.astype(bf)
    d["m_cfg"] = np.zeros((1, 8), np.int32)
    return d


def _bt(B):
    B = np.asarray(B, np.float32)
    e, rows, r = B.shape
    out = np.ascontiguousarray(B.transpose(2, 0, 1).reshape(r, e * rows))
    return out.astype(ml_dtypes.bfloat16)


def kernel(**inputs):
    nc = build_nc(TEST_MODE)
    prep = _prep_inputs(inputs)
    x = np.asarray(inputs["L0_attn_x"])
    wnames = ("L0_ffn_W1", "L0_ffn_W2", "L0_ffn_W3", "L0_ffn_A1", "L0_ffn_A3",
              "L0_attn_Wq", "L0_attn_Wk", "L0_attn_Wv", "L0_attn_Wo",
              "L0_attn_W1", "L0_attn_W2")
    ws = {nm: np.ascontiguousarray(np.asarray(inputs[nm], np.float32)) for nm in wnames}
    in_maps = []
    for c in range(8):
        m = dict(prep)
        m.update(ws)
        role = c % 4
        bb = c // 4
        if role == 3 and TEST_MODE is None:
            xs = np.zeros((T0, DIM), np.float32)
            xs[0:T1] = x[bb, T0:T_, :]
            m["x_in"] = xs
            m["m_maskF"] = prep["m_maskS"]
        else:
            m["x_in"] = np.ascontiguousarray(x[bb, 0:T0, :], np.float32)
        in_maps.append(m)
    res = run_bass_kernel_spmd(nc, in_maps, core_ids=list(range(8)))
    out = np.zeros((B_, T_, DIM), np.float32)
    if TEST_MODE is None:
        for bb in range(B_):
            out[bb, 0:T0] = res.results[4 * bb]["out"]
            out[bb, T0:T_] = res.results[4 * bb + 3]["out"][0:T1]
    else:
        for bb in range(B_):
            out[bb, 0:T0] = res.results[4 * bb]["out"]
    return out
